# version 04 (final)
# baseline (speedup 1.0000x reference)
import os
import contextlib
import numpy as np
import concourse.bass as bass
import concourse.mybir as mybir
from concourse.bass_utils import run_bass_kernel_spmd

F32 = mybir.dt.float32
BF16 = mybir.dt.bfloat16
U8 = mybir.dt.uint8
I32 = mybir.dt.int32
AF = mybir.ActivationFunctionType
ALU = mybir.AluOpType
AX = mybir.AxisListType

ENGINES = ("sync", "tensor", "vector", "scalar", "gpsimd")


class SemT:
    __slots__ = ("h", "n", "name")

    def __init__(self, h, name):
        self.h = h
        self.n = 0
        self.name = name


class OpRec:
    __slots__ = ("op", "a", "k", "post", "needs_inc", "auto_idx", "eng")

    def __init__(self, op, a, k, eng):
        self.op = op
        self.a = a
        self.k = k
        self.post = []
        self.needs_inc = False
        self.auto_idx = None
        self.eng = eng


class AutoWait:
    __slots__ = ("rec",)

    def __init__(self, rec):
        self.rec = rec


class _Post:
    __slots__ = ("post",)

    def __init__(self, post):
        self.post = post

    def then_inc(self, sem, val=1):
        if isinstance(sem, SemT):
            sem.n += val
            self.post.append((sem.h, val))
        else:
            self.post.append((sem, val))
        return self


_WRITE_KEYS = ("out", "accum_out", "dest", "out_max", "out_indices")
_NO_TRACK = ("dma_start", "indirect_dma_start", "dma_start_transpose", "wait_ge",
             "drain", "nop", "memzero", "wait")


def _ap_name(x):
    try:
        return x.name
    except Exception:
        return None


class Rec:

    def __init__(self, kb, eng):
        self._kb = kb
        self._eng = eng
        self._dirty_w = set()
        self._dirty_r = set()
        self._awaited = set()

    def __getattr__(self, name):
        kb = self._kb
        q = kb.q[self._eng]
        eng = self._eng

        def call(*a, **k):
            if name not in _NO_TRACK:
                writes, reads = set(), set()
                for i, x in enumerate(a):
                    n = _ap_name(x)
                    if n is not None:
                        (writes if i == 0 else reads).add(n)
                for key, x in k.items():
                    n = _ap_name(x)
                    if n is not None:
                        (writes if key in _WRITE_KEYS else reads).add(n)
                if eng != "tensor":
                    if (reads & self._dirty_w) or (writes & (self._dirty_w | self._dirty_r)):
                        q.append(OpRec("drain", (), {}, eng))
                        self._dirty_w.clear()
                        self._dirty_r.clear()
                    self._dirty_w |= writes
                    self._dirty_r |= reads
                deps = []
                for n in reads | writes:
                    lw = kb.last_write.get(n)
                    if lw is not None and lw.eng != eng:
                        deps.append(lw)
                for n in writes:
                    for rd in kb.last_readers.get(n, ()):
                        if rd.eng != eng:
                            deps.append(rd)
                for d in deps:
                    if id(d) not in self._awaited:
                        d.needs_inc = True
                        self._awaited.add(id(d))
                        q.append(AutoWait(d))
                rec = OpRec(name, a, k, eng)
                for n in writes:
                    kb.last_write[n] = rec
                    kb.last_readers[n] = []
                for n in reads:
                    kb.last_readers.setdefault(n, []).append(rec)
                q.append(rec)
                return _Post(rec.post)
            rec = OpRec(name, a, k, eng)
            q.append(rec)
            return _Post(rec.post)

        return call

    def wait(self, sem, val=None):
        v = sem.n if val is None else val
        if v > 0:
            self.wait_ge(sem.h, v)
        return v


class KB:
    def __init__(self):
        self.nc = bass.Bass("TRN2", target_bir_lowering=False)
        self.es = contextlib.ExitStack()
        self.q = {e: [] for e in ENGINES}
        self.r = {e: Rec(self, e) for e in ENGINES}
        self._nsem = 0
        self._bar = {}
        self._built = False
        self.prefix = ""
        self._scope = None
        self.last_write = {}
        self.last_readers = {}
        self.adone = {e: self.sem(f"ad_{e}") for e in ENGINES}

    def dram(self, name, shape, dtype=F32, kind="ExternalInput"):
        return self.nc.dram_tensor(self.prefix + name, shape, dtype, kind=kind).ap()

    def _alloc(self, guard):
        h = guard.__enter__()
        if self._scope is not None:
            self._scope.append(guard)
        else:
            self.es.callback(guard.__exit__, None, None, None)
        return h

    def sbuf(self, name, shape, dtype=F32):
        return self._alloc(self.nc.sbuf_tensor(self.prefix + name, shape, dtype))

    def psum(self, name, shape, dtype=F32):
        return self._alloc(self.nc.psum_tensor(self.prefix + name, shape, dtype))

    def scope_begin(self):
        assert self._scope is None
        self._scope = []

    def scope_end(self):
        for g in reversed(self._scope):
            g.__exit__(None, None, None)
        self._scope = None

    def sem(self, name=None):
        self._nsem += 1
        name = self.prefix + (name or f"s{self._nsem}")
        h = self.es.enter_context(self.nc.semaphore(name))
        return SemT(h, name)

    @property
    def sy(self):
        return self.r["sync"]

    @property
    def t(self):
        return self.r["tensor"]

    @property
    def v(self):
        return self.r["vector"]

    @property
    def s(self):
        return self.r["scalar"]

    @property
    def g(self):
        return self.r["gpsimd"]

    def note_write(self, eng, *aps):
        for ap in aps:
            n = _ap_name(ap)
            if n is not None:
                self.last_write.pop(n, None)
                self.last_readers.pop(n, None)

    def barrier(self, engines=ENGINES):
        bars = []
        for e in engines:
            if e not in self._bar:
                self._bar[e] = self.sem(f"bar_{e}")
            b = self._bar[e]
            self.r[e].drain().then_inc(b)
            self.r[e]._dirty_w.clear()
            self.r[e]._dirty_r.clear()
            bars.append((e, b, b.n))
        for e in engines:
            for (oe, ob, ocnt) in bars:
                if oe != e:
                    self.r[e].wait_ge(ob.h, ocnt)
        if set(engines) == set(ENGINES):
            self.last_write.clear()
            self.last_readers.clear()
            for e in ENGINES:
                self.r[e]._awaited.clear()

    def build(self):
        assert not self._built
        self._built = True
        for e in ENGINES:
            cnt = 0
            for rec in self.q[e]:
                if isinstance(rec, OpRec) and rec.needs_inc:
                    cnt += 1
                    rec.auto_idx = cnt
        blk = self.es.enter_context(self.nc.Block())
        adone = self.adone

        def runner(queue, eng_name):
            def run(eng):
                for rec in queue:
                    if isinstance(rec, AutoWait):
                        eng.wait_ge(adone[rec.rec.eng].h, rec.rec.auto_idx)
                        continue
                    try:
                        r = getattr(eng, rec.op)(*rec.a, **rec.k)
                    except Exception as e:
                        raise RuntimeError(
                            f"KB replay failed: {eng_name}.{rec.op} args="
                            f"{[getattr(x, 'shape', x) for x in rec.a]} "
                            f"kwargs={{ {', '.join(f'{k}={getattr(v, 'shape', v)}' for k, v in rec.k.items())} }}"
                        ) from e
                    ups = list(rec.post)
                    if rec.needs_inc:
                        ups.append((adone[eng_name].h, 1))
                    if len(ups) <= 1:
                        for (sem, val) in ups:
                            r.then_inc(sem, val)
                    else:
                        r.then_inc(*ups[0])
                        d = eng.drain()
                        d.then_inc(*ups[1])
                        for (sem, val) in ups[2:]:
                            eng.sem_inc(sem, val)
            return run

        for e in ENGINES:
            if self.q[e]:
                getattr(blk, e)(runner(self.q[e], e))
        self.es.close()
        return self.nc


class DmaRing:

    def __init__(self, kb, name, k):
        self.k = k
        self.s = [kb.sem(f"{name}{i}") for i in range(k)]
        self.cnt = 0

    def issue(self):
        i = self.cnt
        self.cnt += 1
        return self.s[i % self.k], i % self.k

    def wait_slot_free(self, eng_rec, slot_idx):
        i = slot_idx
        if i >= self.k:
            sem = self.s[i % self.k]
            eng_rec.wait_ge(sem.h, sem.n)


class DmaPair:

    def __init__(self, kb, name):
        self.s = [kb.sem(name + "0"), kb.sem(name + "1")]

    def sem(self, m):
        return self.s[m % 2]

    def target(self, m):
        return 16 * (m // 2 + 1)


def bcast_free(ap_2d, times):
    return ap_2d.unsqueeze(2).broadcast_to(list(ap_2d.shape) + [times])


LN2 = float(np.log(2.0))
LN2_HI = float(np.float32(LN2))
LN2_LO = float(np.float32(np.float64(LN2) - np.float64(np.float32(LN2))))
SIX_HI = float(np.float32(1.0 / 6.0))
SIX_LO = float(np.float32(np.float64(1.0) / 6.0 - np.float64(np.float32(1.0 / 6.0))))


class QuantConsts:

    COLS = {"ln2inv": 1.0 / LN2, "negln2": -LN2, "b3ln2": 3.0 * LN2}

    def __init__(self, kb, cst, base):
        self.c = {}
        for i, (name, val) in enumerate(self.COLS.items()):
            col = cst[:, base + i: base + i + 1]
            kb.g.memset(col, val)
            self.c[name] = col


TAU = 2.0 ** -21


def emit_scale_pipeline(kb, bamax, gs6_col, gs_col, sc8, invdq, scr, qc,
                        i32t, eng_v=None, eng_s=None):
    v = eng_v or kb.v
    P, B = bamax.shape
    t0 = scr[:, 0 * B:1 * B]
    t1 = scr[:, 1 * B:2 * B]
    t2 = scr[:, 2 * B:3 * B]
    t3 = scr[:, 3 * B:4 * B]
    v.tensor_scalar(out=t0, in0=bamax, scalar1=gs6_col, scalar2=448.0,
                    op0=ALU.mult, op1=ALU.min)
    v.tensor_scalar(out=t0, in0=t0, scalar1=1e-30, scalar2=2.0 ** -6,
                    op0=ALU.max, op1=ALU.mult)
    v.tensor_scalar(out=t1, in0=bamax, scalar1=0.0, scalar2=64.0,
                    op0=ALU.mult, op1=ALU.add)
    for j in (64, 32, 16, 8, 4, 2, 1):
        thr = 8.0 * 2.0 ** -j
        v.tensor_scalar(out=t2, in0=t0, scalar1=thr, scalar2=float(2.0 ** j),
                        op0=ALU.is_lt, op1=ALU.mult)
        v.tensor_scalar(out=t3, in0=t0, scalar1=thr, scalar2=float(2.0 ** -j),
                        op0=ALU.is_lt, op1=ALU.mult)
        v.tensor_scalar(out=sc8, in0=t0, scalar1=thr, scalar2=None, op0=ALU.is_ge)
        v.tensor_tensor(out=t2, in0=t2, in1=sc8, op=ALU.add)
        v.tensor_tensor(out=t3, in0=t3, in1=sc8, op=ALU.add)
        v.tensor_tensor(out=t0, in0=t0, in1=t2, op=ALU.mult)
        v.tensor_tensor(out=t1, in0=t1, in1=t3, op=ALU.mult)
    v.tensor_scalar(out=t0, in0=t0, scalar1=2.0, scalar2=None, op0=ALU.mult)
    v.tensor_scalar(out=t1, in0=t1, scalar1=0.5, scalar2=None, op0=ALU.mult)
    v.tensor_scalar(out=t2, in0=t0, scalar1=8.5 - TAU, scalar2=None, op0=ALU.is_ge)
    for k in range(10, 17):
        v.scalar_tensor_tensor(out=t2, in0=t0, scalar=(k - 0.5) - TAU, in1=t2,
                               op0=ALU.is_ge, op1=ALU.add)
    v.tensor_scalar(out=t2, in0=t2, scalar1=8.0, scalar2=None, op0=ALU.add)
    for b in (8.5, 10.5, 12.5, 14.5):
        v.tensor_scalar(out=t3, in0=t0, scalar1=b - TAU, scalar2=None, op0=ALU.is_ge)
        v.scalar_tensor_tensor(out=sc8, in0=t0, scalar=b + TAU, in1=t3,
                               op0=ALU.is_le, op1=ALU.mult)
        v.tensor_tensor(out=t2, in0=t2, in1=sc8, op=ALU.subtract)
    v.tensor_tensor(out=sc8, in0=t2, in1=t1, op=ALU.mult)
    v.tensor_scalar(out=t2, in0=sc8, scalar1=1e-30, scalar2=None, op0=ALU.max)
    v.reciprocal(out=t2, in_=t2)
    return v.tensor_scalar(out=invdq, in0=t2, scalar1=gs_col, scalar2=None, op0=ALU.mult)


def np_scale_pipeline(bamax, gs):
    import ml_dtypes
    sc = np.clip(bamax * gs / 6.0, 1e-30, 448.0).astype(np.float32)
    sc8 = sc.astype(ml_dtypes.float8_e4m3fn).astype(np.float32)
    return sc8, gs / sc8


def emit_fp4_ladder(kb, xn, invdq, sc8, xq_out, scr, blk=16, eng_v=None, eng_s=None):
    v = eng_v or kb.v
    s = eng_s or kb.s
    P, N = xn.shape
    B = N // blk
    t = scr[:, 0 * N:1 * N]
    a = scr[:, 1 * N:2 * N]
    u = scr[:, 2 * N:3 * N]
    w = scr[:, 3 * N:4 * N]
    iv3 = bcast_free(invdq, blk)
    x3 = xn.rearrange("p (b k) -> p b k", k=blk)
    t3 = t.rearrange("p (b k) -> p b k", k=blk)
    v.tensor_tensor(out=t3, in0=x3, in1=iv3, op=ALU.mult)
    s.activation(out=a, in_=t, func=AF.Abs)
    v.tensor_scalar(out=u, in0=a, scalar1=0.25, scalar2=None, op0=ALU.is_ge)
    for m in (0.75, 1.25, 1.75):
        v.scalar_tensor_tensor(out=u, in0=a, scalar=m, in1=u, op0=ALU.is_ge, op1=ALU.add)
    v.tensor_scalar(out=w, in0=a, scalar1=2.5, scalar2=None, op0=ALU.is_ge)
    v.scalar_tensor_tensor(out=w, in0=a, scalar=3.5, in1=w, op0=ALU.is_ge, op1=ALU.add)
    v.scalar_tensor_tensor(out=u, in0=u, scalar=0.5, in1=w, op0=ALU.mult, op1=ALU.add)
    v.tensor_scalar(out=w, in0=a, scalar1=5.0, scalar2=2.0, op0=ALU.is_ge, op1=ALU.mult)
    v.tensor_tensor(out=u, in0=u, in1=w, op=ALU.add)
    v.tensor_scalar(out=w, in0=t, scalar1=0.0, scalar2=None, op0=ALU.is_ge)
    v.tensor_scalar(out=w, in0=w, scalar1=2.0, scalar2=1.0, op0=ALU.mult, op1=ALU.subtract)
    w3 = w.rearrange("p (b k) -> p b k", k=blk)
    v.tensor_tensor(out=w3, in0=w3, in1=bcast_free(sc8, blk), op=ALU.mult)
    return v.tensor_tensor(out=xq_out, in0=u, in1=w, op=ALU.mult)


def np_fp4_ladder(xn, invdq, sc8, blk=16):
    P, N = xn.shape
    x3 = xn.reshape(P, N // blk, blk)
    t = x3 * invdq[:, :, None]
    a = np.abs(t)
    VALS = np.array([0.0, .5, 1, 1.5, 2, 3, 4, 6], np.float32)
    MIDS = np.array([.25, .75, 1.25, 1.75, 2.5, 3.5, 5.0], np.float32)
    q = VALS[np.digitize(a, MIDS)]
    sgn = np.where(t >= 0, 1.0, -1.0).astype(np.float32)
    return (q * sgn * sc8[:, :, None]).reshape(P, N).astype(np.float32)


class _QuantMod:
    QuantConsts = QuantConsts
    emit_scale_pipeline = staticmethod(emit_scale_pipeline)
    emit_fp4_ladder = staticmethod(emit_fp4_ladder)
    SIX_HI = SIX_HI
    SIX_LO = SIX_LO
    LN2_HI = LN2_HI
    LN2_LO = LN2_LO
    TAU = TAU


quant = _QuantMod

def _mk_s1():
    P = 128
    TLOC = 4096
    NT = TLOC // P
    D = 1024
    DT = D // P
    O2 = 2048
    SEG = 2048
    PAD = 7
    SEGW = SEG + 2 * PAD
    XGW = 2 * SEGW


    import os
    UPTO = int(os.environ.get('S1_UPTO', '5'))


    def build():
        kb = KB()
        dio = {"x": kb.dram("x", [TLOC, D]), "W1": kb.dram("W1", [D, O2]),
               "b1": kb.dram("b1", [O2]), "W2": kb.dram("W2", [D, D]),
               "b2": kb.dram("b2", [D]), "csw": kb.dram("csw", [16, 15]),
               "s0": kb.dram("s0", [TLOC, D], kind="ExternalOutput"),
               "xn": kb.dram("xn", [TLOC, D], kind="ExternalOutput"),
               "xg": kb.dram("xg", [P, DT * XGW], BF16, kind="ExternalOutput"),
               "cv": kb.dram("cv", [P, DT * TLOC], BF16, kind="ExternalOutput")}
        emit(kb, dio, dumps=True)
        return kb


    def emit(kb, dio, dumps=False):
        nc = kb.nc
        x = dio["x"]; W1 = dio["W1"]; b1 = dio["b1"]; W2 = dio["W2"]
        b2 = dio["b2"]; csw = dio["csw"]
        s0_o = dio["s0"]; xn_o = dio["xn"]
        xg_o = dio.get("xg"); cv_o = dio.get("cv")
        wsm_h = kb.dram("wsm", [16, 15], kind="Internal")

        xt = kb.sbuf("xt", [P, 2 * D])
        sq = kb.sbuf("sq", [P, D])
        st = kb.sbuf("st", [P, 2 * 8])
        cst = kb.sbuf("cst", [P, 4])
        xnb = kb.sbuf("xnb", [P, 2 * D], BF16)
        xnf = kb.sbuf("xnf", [P, 2 * D])
        featA = kb.sbuf("featA", [P, DT * TLOC], BF16)
        xgF = kb.sbuf("xgF", [P, DT * XGW], BF16)
        w1s = xnf[:, 0:8 * P]
        w1b = kb.sbuf("w1b", [P, 2 * 8 * P], BF16)
        sg = kb.sbuf("sg", [P, 2 * 512])
        w2s = kb.sbuf("w2s", [P, 512])
        w2b = kb.sbuf("w2b", [P, DT * D], BF16)
        sout = xnf[:, D:2 * D]
        b1f = kb.sbuf("b1f", [1, O2])
        b1b = kb.sbuf("b1b", [1, O2], BF16)
        b2f = kb.sbuf("b2f", [1, D])
        b2b = kb.sbuf("b2b", [1, D], BF16)
        onesr = kb.sbuf("onesr", [1, 512], BF16)
        ones128 = kb.sbuf("ones128", [P, P], BF16)
        identb = kb.sbuf("identb", [P, P], BF16)
        cswT = kb.sbuf("cswT", [16, 16])
        smx = kb.sbuf("smx", [16, 4])
        expT = kb.sbuf("expT", [16, 16])
        fcolf = kb.sbuf("fcolf", [P, DT * 16])
        fcolb = kb.sbuf("fcolb", [P, DT * 16], BF16)
        fbc = kb.sbuf("fbc", [P, P], BF16)
        diagb = kb.sbuf("diagb", [P, 15 * P], BF16)

        psA = kb.psum("psA", [P, 512])
        psB = kb.psum("psB", [P, 512])
        psC = kb.psum("psC", [P, 512])
        psD = kb.psum("psD", [P, 512])
        psT = [kb.psum("psT0", [P, P], BF16), kb.psum("psT1", [P, P], BF16)]

        sy, t, v, s, g = kb.sy, kb.t, kb.v, kb.s, kb.g

        ld = kb.sem("ld")
        xld = DmaPair(kb, "xld")
        lnv = kb.sem("lnv")
        lns = kb.sem("lns")
        lnt = kb.sem("lnt")
        trv = kb.sem("trv")
        xns = DmaPair(kb, "xns")
        wdma = kb.sem("wdma")
        wcast = kb.sem("wcast")
        hsem = kb.sem("hsem")
        ssg = kb.sem("ssg")
        xgv = kb.sem("xgv")
        smv = kb.sem("smv")
        dgs = kb.sem("dgs")
        convm = kb.sem("convm")
        convs = kb.sem("convs")
        w2m = kb.sem("w2m")
        w2v = kb.sem("w2v")
        sst = kb.sem("sst")
        fin = kb.sem("fin") if dumps else None
        xcv = kb.sem("xcv")

        sy.dma_start(out=cswT[0:16, 0:15], in_=csw).then_inc(ld, 16)
        sy.dma_start(out=b1f[:], in_=b1.unsqueeze(0)).then_inc(ld, 16)
        sy.dma_start(out=b2f[:], in_=b2.unsqueeze(0)).then_inc(ld, 16)
        for j in range(DT):
            for h2 in range(2):
                sy.wait(wcast) if (j or h2) else None
                sy.dma_start(out=w2s[:], in_=W2[j * P:(j + 1) * P, h2 * 512:(h2 + 1) * 512]).then_inc(ld, 16)
                v.wait_ge(ld.h, ld.n)
                v.tensor_copy(out=w2b[:, j * D + h2 * 512: j * D + (h2 + 1) * 512], in_=w2s[:]).then_inc(wcast, 1)
        g.memset(cst[:, 0:1], 1e-5)
        g.memset(cst[:, 1:2], -0.5)
        g.memset(onesr[:], 1.0)
        g.memset(ones128[:], 1.0)
        g.memset(xgF[:], 0.0)
        g.memset(fcolf[:], 0.0)
        g.affine_select(identb[:], ones128[:], [[1, P]], ALU.is_equal, 0.0,
                        base=0, channel_multiplier=-1)
        v.wait_ge(ld.h, ld.n)
        v.tensor_copy(out=b1b[:], in_=b1f[:])
        v.tensor_copy(out=b2b[:], in_=b2f[:]).then_inc(wcast, 1)
        kb.barrier()

        if UPTO < 1:
            return
        v.reduce_max(out=smx[0:16, 0:1], in_=cswT[0:16, 0:15], axis=AX.X)
        v.tensor_scalar(out=smx[0:16, 1:2], in0=smx[0:16, 0:1], scalar1=-1.0, scalar2=None, op0=ALU.mult).then_inc(smv, 1)
        s.wait_ge(smv.h, 1)
        s.activation(out=expT[0:16, 0:15], in_=cswT[0:16, 0:15], func=AF.Exp, bias=smx[0:16, 1:2]).then_inc(smv, 1)
        v.wait_ge(smv.h, 2)
        v.reduce_sum(out=smx[0:16, 2:3], in_=expT[0:16, 0:15], axis=AX.X)
        v.reciprocal(out=smx[0:16, 3:4], in_=smx[0:16, 2:3])
        v.tensor_scalar(out=expT[0:16, 0:15], in0=expT[0:16, 0:15], scalar1=smx[0:16, 3:4], scalar2=None, op0=ALU.mult).then_inc(smv, 1)
        sy.wait_ge(smv.h, 3)
        sy.dma_start(out=wsm_h, in_=expT[0:16, 0:15]).then_inc(ld, 16)
        sy.wait_ge(ld.h, ld.n)
        for dt in range(DT):
            for rr in range(2):
                sy.dma_start(out=fcolf[64 * rr:64 * rr + 64, dt * 16: dt * 16 + 15],
                             in_=wsm_h[2 * dt + rr:2 * dt + rr + 1, :].broadcast_to([64, 15])).then_inc(ld, 16)
        v.wait_ge(ld.h, ld.n)
        v.tensor_copy(out=fcolb[:], in_=fcolf[:]).then_inc(smv, 1)
        kb.barrier()

        if UPTO < 2:
            return
        for m in range(NT):
            par = m % 2
            xtp = xt[:, par * D:(par + 1) * D]
            stp = st[:, par * 8:(par + 1) * 8]
            xnp = xnb[:, par * D:(par + 1) * D]
            if m >= 2:
                sy.wait_ge(lns.h, 3 * (m - 2) + 3)
            sy.dma_start(out=xtp, in_=x[m * P:(m + 1) * P, :]).then_inc(xld.sem(m), 16)
            v.wait_ge(xld.sem(m).h, xld.target(m))
            if m >= 2:
                v.wait_ge(lns.h, 3 * (m - 2) + 3)
            v.reduce_sum(out=stp[:, 0:1], in_=xtp, axis=AX.X)
            v.tensor_scalar(out=stp[:, 1:2], in0=stp[:, 0:1], scalar1=-1.0 / D, scalar2=None, op0=ALU.mult).then_inc(lnv, 1)
            s.wait_ge(lnv.h, 3 * m + 1)
            s.activation(out=sq[:], in_=xtp, func=AF.Square, bias=stp[:, 1:2], accum_out=stp[:, 2:3]).then_inc(lns, 1)
            v.wait_ge(lns.h, 3 * m + 1)
            v.tensor_scalar(out=stp[:, 3:4], in0=stp[:, 2:3], scalar1=1.0 / D, scalar2=None, op0=ALU.mult).then_inc(lnv, 1)
            s.wait_ge(lnv.h, 3 * m + 2)
            s.activation(out=stp[:, 4:5], in_=stp[:, 3:4], func=AF.Ln, bias=cst[:, 0:1])
            s.activation(out=stp[:, 5:6], in_=stp[:, 4:5], func=AF.Exp, scale=cst[:, 1:2]).then_inc(lns, 1)
            v.wait_ge(lns.h, 3 * m + 2)
            v.tensor_tensor(out=stp[:, 6:7], in0=stp[:, 1:2], in1=stp[:, 5:6], op=ALU.mult).then_inc(lnv, 1)
            s.wait_ge(lnv.h, 3 * m + 3)
            if m >= 2:
                s.wait_ge(xns.sem(m - 2).h, xns.target(m - 2))
                s.wait_ge(xcv.h, m - 1)
            xfp = xnf[:, par * D:(par + 1) * D]
            s.activation(out=xfp, in_=xtp, func=AF.Identity, scale=stp[:, 5:6], bias=stp[:, 6:7]).then_inc(lns, 1)
            g.wait_ge(lns.h, 3 * m + 3)
            g.dma_start(out=xn_o[m * P:(m + 1) * P, :], in_=xfp).then_inc(xns.sem(m), 16)
            v.wait_ge(lns.h, 3 * m + 3)
            if m >= 2:
                v.wait_ge(trv.h, 8 * (m - 1))
            v.tensor_copy(out=xnp, in_=xfp).then_inc(xcv, 1)
            t.wait_ge(xcv.h, m + 1)
            for j in range(DT):
                k = 8 * m + j
                if k >= 2:
                    t.wait_ge(trv.h, k - 1)
                t.transpose(psT[k % 2][:], xnp[:, j * P:(j + 1) * P], identb[:]).then_inc(lnt, 1)
                v.wait_ge(lnt.h, k + 1)
                v.tensor_copy(out=featA[:, j * TLOC + m * P: j * TLOC + (m + 1) * P], in_=psT[k % 2][:]).then_inc(trv, 1)
        kb.barrier()

        if UPTO < 3:
            return
        sy.wait_ge(xns.s[0].h, xns.s[0].n)
        sy.wait_ge(xns.s[1].h, xns.s[1].n)
        def xg_cols(c):
            if c < 4:
                return PAD + 512 * c
            return SEGW + PAD + 512 * (c - 4)

        for j in range(DT):
            for half, o in ((0, j + 8), (1, j)):
                if wcast.n > 0:
                    sy.wait_ge(wcast.h, wcast.n)
                for K in range(8):
                    sy.dma_start(out=w1s[:, K * P:(K + 1) * P],
                                 in_=W1[K * P:(K + 1) * P, o * P:(o + 1) * P]).then_inc(wdma, 16)
                if j > 0:
                    v.wait_ge(hsem.h, (j - 1) * 16 + 16)
                v.wait_ge(wdma.h, wdma.n)
                for K in range(8):
                    v.tensor_copy(out=w1b[:, (half * 8 + K) * P:(half * 8 + K + 1) * P],
                                  in_=w1s[:, K * P:(K + 1) * P]).then_inc(wcast, 1)
            base = j * 16
            for c in range(8):
                ps_g = psA if c % 2 == 0 else psB
                t.wait_ge(wcast.h, j * 16 + 8 + 17)
                if c >= 2:
                    t.wait_ge(ssg.h, j * 8 + (c - 2) + 1)
                for K in range(8):
                    t.matmul(ps_g[:], w1b[:, K * P:(K + 1) * P], featA[:, K * TLOC + c * 512: K * TLOC + (c + 1) * 512],
                             start=(K == 0), stop=False)
                t.matmul(ps_g[:], b1b[0:1, (j + 8) * P:(j + 9) * P], onesr[0:1, :], start=False, stop=True).then_inc(hsem, 1)
                s.wait_ge(hsem.h, base + 2 * c + 1)
                if c >= 2:
                    s.wait_ge(xgv.h, j * 8 + c - 1)
                s.activation(out=sg[:, (c % 2) * 512:(c % 2 + 1) * 512], in_=ps_g[:], func=AF.Sigmoid).then_inc(ssg, 1)
                ps_a = psC if c % 2 == 0 else psD
                t.wait_ge(wcast.h, j * 16 + 16 + 17)
                if c >= 2:
                    t.wait_ge(xgv.h, j * 8 + c - 1)
                for K in range(8):
                    t.matmul(ps_a[:], w1b[:, (8 + K) * P:(9 + K) * P], featA[:, K * TLOC + c * 512: K * TLOC + (c + 1) * 512],
                             start=(K == 0), stop=False)
                t.matmul(ps_a[:], b1b[0:1, j * P:(j + 1) * P], onesr[0:1, :], start=False, stop=True).then_inc(hsem, 1)
                v.wait_ge(hsem.h, base + 2 * c + 2)
                v.wait_ge(ssg.h, j * 8 + c + 1)
                cc = xg_cols(c)
                v.scalar_tensor_tensor(out=xgF[:, j * XGW + cc: j * XGW + cc + 512], in0=ps_a[:], scalar=1.0,
                                       in1=sg[:, (c % 2) * 512:(c % 2 + 1) * 512],
                                       op0=ALU.mult, op1=ALU.mult).then_inc(xgv, 1)
        kb.barrier()
        if dumps:
            sy.dma_start(out=xg_o, in_=xgF[:]).then_inc(fin, 16)

        if UPTO < 4:
            return
        for dt in range(DT):
            if dt > 0:
                g.wait_ge(convm.h, 8 * dt)
            for k in range(15):
                if dgs.n > 0:
                    v.wait_ge(dgs.h, dgs.n)
                v.tensor_copy(out=fbc[:], in_=bcast_free(fcolb[:, dt * 16 + k: dt * 16 + k + 1], P).squeeze(1)).then_inc(dgs, 1)
                g.wait_ge(dgs.h, dgs.n)
                g.affine_select(diagb[:, k * P:(k + 1) * P], fbc[:], [[1, P]], ALU.is_equal, 0.0,
                                base=0, channel_multiplier=-1).then_inc(dgs, 1)
            for c in range(8):
                ps_c = psA if c % 2 == 0 else psB
                t.wait_ge(dgs.h, dgs.n)
                if dt * 8 + c >= 2:
                    t.wait_ge(convs.h, dt * 8 + c - 1)
                base_col = dt * XGW + (512 * c if c < 4 else SEGW + 512 * (c - 4))
                for k in range(15):
                    mm = t.matmul(ps_c[:], diagb[:, k * P:(k + 1) * P], xgF[:, base_col + k: base_col + k + 512],
                                  start=(k == 0), stop=(k == 14))
                    if k == 14:
                        mm.then_inc(convm, 1)
                s.wait_ge(convm.h, dt * 8 + c + 1)
                s.copy(out=featA[:, dt * TLOC + c * 512: dt * TLOC + (c + 1) * 512], in_=ps_c[:]).then_inc(convs, 1)
        kb.barrier()
        if dumps:
            sy.dma_start(out=cv_o, in_=featA[:]).then_inc(fin, 16)

        if UPTO < 5:
            return
        for m in range(NT):
            par = m % 2
            for half in range(2):
                ps_w = (psC, psD)[half]
                if m >= 1:
                    t.wait_ge(w2v.h, 2 * (m - 1) + half + 1)
                for K in range(DT):
                    t.matmul(ps_w[:], featA[:, K * TLOC + m * P:(K * TLOC + (m + 1) * P)],
                             w2b[:, K * D + half * 512: K * D + (half + 1) * 512],
                             start=(K == 0), stop=False)
                t.matmul(ps_w[:], onesr[0:1, 0:P], b2b[0:1, half * 512:(half + 1) * 512], start=False, stop=True).then_inc(w2m, 1)
                v.wait_ge(w2m.h, 2 * m + half + 1)
                if m >= 1:
                    v.wait_ge(sst.h, 16 * m)
                v.tensor_copy(out=sout[:, half * 512:(half + 1) * 512], in_=ps_w[:]).then_inc(w2v, 1)
            sy.wait_ge(w2v.h, 2 * m + 2)
            sy.dma_start(out=s0_o[m * P:(m + 1) * P, :], in_=sout[:]).then_inc(sst, 16)
        sy.wait_ge(sst.h, sst.n)
        if dumps:
            sy.wait_ge(fin.h, fin.n)
        sy.wait_ge(xns.s[0].h, xns.s[0].n)
        sy.wait_ge(xns.s[1].h, xns.s[1].n)
        return

    class _NS:
        pass
    _NS.build = staticmethod(build)
    _NS.emit = staticmethod(emit)
    return _NS

def _mk_s2():
    P = 128
    TLOC = 4096
    NT = 32
    D = 1024
    DT = 8
    OI = 4096
    OT = 32
    UPTO = int(os.environ.get("S2_UPTO", "9"))


    def build():
        kb = KB()
        dio = {"xn": kb.dram("xn", [TLOC, D]), "gw": kb.dram("gw", [OI, D]),
               "uw": kb.dram("uw", [OI, D]), "dw": kb.dram("dw", [D, OI]),
               "sin": kb.dram("sin", [1]), "sind": kb.dram("sind", [1]),
               "g0": kb.dram("g0", [TLOC, D], kind="ExternalOutput")}
        emit(kb, dio)
        return kb


    def emit(kb, dio):
        xn_i = dio["xn"]; gw_i = dio["gw"]; uw_i = dio["uw"]; dw_i = dio["dw"]
        sin_i = dio["sin"]; sind_i = dio["sind"]; g0_o = dio["g0"]
        wg_h = kb.dram("wg_h", [D, OI], BF16, kind="Internal")
        wu_h = kb.dram("wu_h", [D, OI], BF16, kind="Internal")
        wd_h = kb.dram("wd_h", [OI, D], BF16, kind="Internal")
        hq_h = kb.dram("hq_h", [OI, TLOC], BF16, kind="ExternalOutput" if os.environ.get("S2_DBG") else "Internal")
        gs_h = kb.dram("gs_h", [4], kind="Internal")

        featB = kb.sbuf("featB", [P, 32 * 1024], BF16)
        wstage = kb.sbuf("wstage", [P, 2 * OI])
        wqb = kb.sbuf("wqb", [P, OI], BF16)
        trev = kb.sbuf("trev", [P, 4 * P], BF16)
        scrN = kb.sbuf("scrN", [P, 4 * 1024])
        scrB = kb.sbuf("scrB", [P, 4 * 256])
        bam = kb.sbuf("bam", [P, 256])
        sc8 = kb.sbuf("sc8", [P, 256])
        invdq = kb.sbuf("invdq", [P, 256])
        i32t = kb.sbuf("i32t", [P, 256], I32)
        gmaxc = kb.sbuf("gmaxc", [P, 4])
        gsm = kb.sbuf("gsm", [P, 10])
        cst = kb.sbuf("cst", [P, 8])
        xnt = kb.sbuf("xnt", [P, 2 * D])
        xqt = kb.sbuf("xqt", [P, D], BF16)
        wgc = kb.sbuf("wgc", [P, 8 * 512], BF16)
        wuc = kb.sbuf("wuc", [P, 8 * 512], BF16)
        sgt = kb.sbuf("sgt", [P, 512])
        hid = kb.sbuf("hid", [P, 512])
        hqt = kb.sbuf("hqt", [P, 512], BF16)
        hqs = kb.sbuf("hqs", [P, 2 * 4096], BF16)
        gout = kb.sbuf("gout", [P, D])
        identb = kb.sbuf("identb", [P, P], BF16)
        identf = kb.sbuf("identf", [P, P])
        ones128 = kb.sbuf("ones128", [P, P], BF16)

        psA = kb.psum("psA", [P, 512])
        psB = kb.psum("psB", [P, 512])
        psC = kb.psum("psC", [P, 512])
        psD = kb.psum("psD", [P, 512])
        psG = kb.psum("psG", [P, P])
        psT = [kb.psum("psT0", [P, P], BF16), kb.psum("psT1", [P, P], BF16)]

        sy, t, v, s, g = kb.sy, kb.t, kb.v, kb.s, kb.g
        qc = quant.QuantConsts(kb, cst, 0)
        g.memset(ones128[:], 1.0)
        g.affine_select(identb[:], ones128[:], [[1, P]], ALU.is_equal, 0.0,
                        base=0, channel_multiplier=-1)
        v.tensor_copy(out=identf[:], in_=identb[:])

        ld = kb.sem("ld")
        wld = DmaPair(kb, "wld")
        wre = kb.sem("wre")
        tring = DmaRing(kb, "tst", 4)
        xld = DmaPair(kb, "xld")
        wcl = kb.sem("wcl")
        hld = DmaPair(kb, "hld")
        gst = kb.sem("gst")
        tev = kb.sem("tev")

        wtile = [0]

        def stage_load(w_i, rt, ncol):
            tix = wtile[0]
            wtile[0] += 1
            stg = wstage[:, (tix % 2) * OI:(tix % 2) * OI + ncol]
            if tix >= 2:
                sy.wait_ge(wre.h, tix - 1)
            sy.dma_start(out=stg, in_=w_i[rt * P:(rt + 1) * P, :]).then_inc(
                wld.sem(tix), 16)
            v.wait_ge(wld.sem(tix).h, wld.target(tix))
            return stg

        sy.dma_start(out=gsm[0:1, 3:4], in_=sin_i.unsqueeze(0)).then_inc(ld, 16)
        sy.dma_start(out=gsm[0:1, 4:5], in_=sind_i.unsqueeze(0)).then_inc(ld, 16)
        g.memset(gmaxc[:, 0:4], 0.0)
        kb.barrier()
        for wi, (w_i, ntile, ncol) in enumerate(((gw_i, OT, D), (uw_i, OT, D), (dw_i, DT, OI))):
            for rt in range(ntile):
                stg = stage_load(w_i, rt, ncol)
                v.tensor_reduce(out=gmaxc[:, 3:4], in_=stg, axis=AX.X, op=ALU.max,
                                apply_absolute_value=True)
                v.tensor_tensor(out=gmaxc[:, wi:wi + 1], in0=gmaxc[:, wi:wi + 1],
                                in1=gmaxc[:, 3:4], op=ALU.max).then_inc(wre, 1)
        kb.barrier()
        gsv = kb.sem("gsv")
        t.transpose(psG[0:4, :], gmaxc[:, 0:4], identf[:])
        v.tensor_reduce(out=gmaxc[0:4, 0:1], in_=psG[0:4, 0:P], axis=AX.X, op=ALU.max)
        v.reciprocal(out=gmaxc[0:4, 1:2], in_=gmaxc[0:4, 0:1])
        v.tensor_scalar(out=gmaxc[0:4, 2:3], in0=gmaxc[0:4, 1:2], scalar1=2688.0,
                        scalar2=None, op0=ALU.mult).then_inc(gsv, 1)
        sy.wait_ge(gsv.h, 1)
        sy.dma_start(out=gs_h[0:3].unsqueeze(1), in_=gmaxc[0:3, 2:3]).then_inc(ld, 16)
        sy.wait_ge(ld.h, ld.n)
        for k in range(3):
            sy.dma_start(out=gsm[:, k:k + 1], in_=gs_h[k:k + 1].unsqueeze(0).broadcast_to([P, 1])).then_inc(ld, 16)
        kb.barrier()
        sy.dma_start(out=gs_h[3:4].unsqueeze(1), in_=gsm[0:1, 3:4]).then_inc(ld, 16)
        sy.wait_ge(ld.h, ld.n)
        sy.dma_start(out=gsm[:, 3:4], in_=gs_h[3:4].unsqueeze(0).broadcast_to([P, 1])).then_inc(ld, 16)
        sy.wait_ge(ld.h, ld.n)
        sy.dma_start(out=gs_h[3:4].unsqueeze(1), in_=gsm[0:1, 4:5]).then_inc(ld, 16)
        sy.wait_ge(ld.h, ld.n)
        sy.dma_start(out=gsm[:, 4:5], in_=gs_h[3:4].unsqueeze(0).broadcast_to([P, 1])).then_inc(ld, 16)
        sy.wait_ge(ld.h, ld.n)
        kb.note_write(None, gsm)
        kb.barrier()
        if UPTO < 1:
            return

        def quant_weight(w_i, gs_ci, ntile, ncol, out_h):
            v.tensor_scalar(out=gsm[:, 8:9], in0=gsm[:, gs_ci:gs_ci + 1],
                            scalar1=quant.SIX_LO, scalar2=None, op0=ALU.mult)
            v.scalar_tensor_tensor(out=gsm[:, 6:7], in0=gsm[:, gs_ci:gs_ci + 1],
                                   scalar=quant.SIX_HI, in1=gsm[:, 8:9],
                                   op0=ALU.mult, op1=ALU.add)
            for rt in range(ntile):
                stg = stage_load(w_i, rt, ncol)
                last = None
                for ch in range(ncol // 1024):
                    cb = ch * 64
                    v.tensor_reduce(out=bam[:, cb:cb + 64],
                                    in_=stg[:, ch * 1024:(ch + 1) * 1024].rearrange("p (b k) -> p b k", k=16),
                                    axis=AX.X, op=ALU.max, apply_absolute_value=True)
                    quant.emit_scale_pipeline(kb, bam[:, cb:cb + 64], gsm[:, 6:7],
                                              gsm[:, gs_ci:gs_ci + 1],
                                              sc8[:, cb:cb + 64], invdq[:, cb:cb + 64],
                                              scrB[:], qc, i32t[:, 0:64])
                    last = quant.emit_fp4_ladder(kb, stg[:, ch * 1024:(ch + 1) * 1024],
                                                 invdq[:, cb:cb + 64], sc8[:, cb:cb + 64],
                                                 wqb[:, ch * 1024:(ch + 1) * 1024], scrN[:])
                last.then_inc(wre, 1)
                for b in range(ncol // P):
                    sem, slot = tring.issue()
                    t.transpose(psT[slot % 2][:], wqb[:, b * P:(b + 1) * P], identb[:])
                    ev = trev[:, slot * P:(slot + 1) * P]
                    tring.wait_slot_free(kb.s, tring.cnt - 1)
                    s.copy(out=ev, in_=psT[slot % 2][:]).then_inc(tev, 1)
                    sy.wait_ge(tev.h, tev.n)
                    sy.dma_start(out=out_h[b * P:(b + 1) * P, rt * P:(rt + 1) * P],
                                 in_=ev).then_inc(sem, 16)

        quant_weight(gw_i, 0, OT, D, wg_h)
        quant_weight(uw_i, 1, OT, D, wu_h)
        quant_weight(dw_i, 2, DT, OI, wd_h)
        kb.barrier()
        for sm in tring.s:
            sy.wait_ge(sm.h, sm.n)
        if UPTO < 2:
            return

        v.tensor_scalar(out=gsm[:, 8:9], in0=gsm[:, 3:4], scalar1=quant.SIX_LO,
                        scalar2=None, op0=ALU.mult)
        v.scalar_tensor_tensor(out=gsm[:, 6:7], in0=gsm[:, 3:4], scalar=quant.SIX_HI,
                               in1=gsm[:, 8:9], op0=ALU.mult, op1=ALU.add)
        for m in range(NT):
            xt = xnt[:, (m % 2) * D:(m % 2 + 1) * D]
            if m >= 2:
                sy.wait_ge(wre.h, wre.n - 1)
            sy.dma_start(out=xt, in_=xn_i[m * P:(m + 1) * P, :]).then_inc(xld.sem(m), 16)
            v.wait_ge(xld.sem(m).h, xld.target(m))
            v.tensor_reduce(out=bam[:, 0:64], in_=xt.rearrange("p (b k) -> p b k", k=16),
                            axis=AX.X, op=ALU.max, apply_absolute_value=True)
            quant.emit_scale_pipeline(kb, bam[:, 0:64], gsm[:, 6:7], gsm[:, 3:4],
                                      sc8[:, 0:64], invdq[:, 0:64], scrB[:], qc, i32t[:, 0:64])
            quant.emit_fp4_ladder(kb, xt, invdq[:, 0:64], sc8[:, 0:64], xqt[:],
                                  scrN[:]).then_inc(wre, 1)
            for b in range(DT):
                slot = (m * DT + b) % 2
                t.transpose(psT[slot][:], xqt[:, b * P:(b + 1) * P], identb[:])
                v.tensor_copy(out=featB[:, (b * NT + m) * P:(b * NT + m + 1) * P],
                              in_=psT[slot][:])
        kb.barrier()
        if os.environ.get("S2_DBG"):
            xq_dbg = kb.dram("xq_dbg", [P, 32 * 1024], BF16, kind="ExternalOutput")
            dbs = kb.sem("dbs")
            sy.dma_start(out=xq_dbg, in_=featB[:]).then_inc(dbs, 16)
            sy.wait_ge(dbs.h, 16)
        if UPTO < 3:
            return

        v.tensor_scalar(out=gsm[:, 8:9], in0=gsm[:, 4:5], scalar1=quant.SIX_LO,
                        scalar2=None, op0=ALU.mult)
        v.scalar_tensor_tensor(out=gsm[:, 6:7], in0=gsm[:, 4:5], scalar=quant.SIX_HI,
                               in1=gsm[:, 8:9], op0=ALU.mult, op1=ALU.add)
        v.tensor_tensor(out=gsm[:, 5:6], in0=gsm[:, 3:4], in1=gsm[:, 0:1], op=ALU.mult)
        v.reciprocal(out=gsm[:, 5:6], in_=gsm[:, 5:6])
        v.tensor_tensor(out=gsm[:, 7:8], in0=gsm[:, 3:4], in1=gsm[:, 1:2], op=ALU.mult)
        v.reciprocal(out=gsm[:, 7:8], in_=gsm[:, 7:8])
        gumm = kb.sem("gumm")
        if os.environ.get("S2_DBG2"):
            bam_dbg = kb.dram("bam_dbg", [P, 256 * 32], kind="ExternalOutput")
            sc8_dbg = kb.dram("sc8_dbg", [P, 256 * 32], kind="ExternalOutput")
            dbg2s = DmaPair(kb, "dbg2s")
            dbgv = kb.sem("dbgv")
        for oc in range(8):
            if oc > 0:
                sy.wait_ge(gumm.h, oc)
            for K in range(DT):
                sy.dma_start(out=wgc[:, K * 512:(K + 1) * 512],
                             in_=wg_h[K * P:(K + 1) * P, oc * 512:(oc + 1) * 512]).then_inc(wcl, 16)
                sy.dma_start(out=wuc[:, K * 512:(K + 1) * 512],
                             in_=wu_h[K * P:(K + 1) * P, oc * 512:(oc + 1) * 512]).then_inc(wcl, 16)
            t.wait_ge(wcl.h, wcl.n)
            for m in range(NT):
                ps_g = psA if m % 2 == 0 else psB
                ps_u = psC if m % 2 == 0 else psD
                for K in range(DT):
                    t.matmul(ps_g[:], featB[:, (K * NT + m) * P:(K * NT + m + 1) * P],
                             wgc[:, K * 512:(K + 1) * 512], start=(K == 0), stop=(K == DT - 1))
                for K in range(DT):
                    mmu = t.matmul(ps_u[:], featB[:, (K * NT + m) * P:(K * NT + m + 1) * P],
                                   wuc[:, K * 512:(K + 1) * 512], start=(K == 0), stop=(K == DT - 1))
                    if m == NT - 1 and K == DT - 1:
                        mmu.then_inc(gumm, 1)
                s.activation(out=sgt[:], in_=ps_g[:], func=AF.Silu, scale=gsm[:, 5:6])
                v.scalar_tensor_tensor(out=hid[:], in0=ps_u[:], scalar=gsm[:, 7:8], in1=sgt[:],
                                       op0=ALU.mult, op1=ALU.mult)
                v.tensor_reduce(out=bam[:, 0:32], in_=hid[:].rearrange("p (b k) -> p b k", k=16),
                                axis=AX.X, op=ALU.max, apply_absolute_value=True)
                quant.emit_scale_pipeline(kb, bam[:, 0:32], gsm[:, 6:7], gsm[:, 4:5],
                                          sc8[:, 0:32], invdq[:, 0:32], scrB[:], qc, i32t[:, 0:32])
                quant.emit_fp4_ladder(kb, hid[:], invdq[:, 0:32], sc8[:, 0:32], hqt[:], scrN[:])
                if os.environ.get("S2_DBG2"):
                    it = oc * 32 + m
                    dsm = dbg2s.sem(it)
                    v.drain().then_inc(dbgv, 1)
                    sy.wait_ge(dbgv.h, dbgv.n)
                    sy.dma_start(out=bam_dbg[:, it * 32:(it + 1) * 32], in_=bam[:, 0:32]).then_inc(dsm, 16)
                    sy.dma_start(out=sc8_dbg[:, it * 32:(it + 1) * 32], in_=sc8[:, 0:32]).then_inc(dsm, 16)
                    v.wait_ge(dsm.h, 32 * (it // 2 + 1))
                for b in range(4):
                    sem, slot = tring.issue()
                    t.transpose(psT[slot % 2][:], hqt[:, b * P:(b + 1) * P], identb[:])
                    ev = trev[:, slot * P:(slot + 1) * P]
                    tring.wait_slot_free(kb.s, tring.cnt - 1)
                    s.copy(out=ev, in_=psT[slot % 2][:]).then_inc(tev, 1)
                    sy.wait_ge(tev.h, tev.n)
                    sy.dma_start(out=hq_h[oc * 512 + b * P: oc * 512 + (b + 1) * P,
                                          m * P:(m + 1) * P], in_=ev).then_inc(sem, 16)
        kb.barrier()
        for sm in tring.s:
            sy.wait_ge(sm.h, sm.n)
        if UPTO < 4:
            return

        for Ko in range(OT):
            sy.dma_start(out=featB[:, Ko * D:(Ko + 1) * D],
                         in_=wd_h[Ko * P:(Ko + 1) * P, :]).then_inc(wcl, 16)
        v.tensor_tensor(out=gsm[:, 5:6], in0=gsm[:, 4:5], in1=gsm[:, 2:3], op=ALU.mult)
        v.reciprocal(out=gsm[:, 5:6], in_=gsm[:, 5:6])
        t.wait_ge(wcl.h, wcl.n)
        for m in range(NT):
            hs = hqs[:, (m % 2) * 4096:(m % 2 + 1) * 4096]
            if m >= 2:
                sy.wait_ge(gst.h, 16 * (m - 1))
            for part4 in range(4):
                sy.dma_start(out=hs[:, part4 * 1024:(part4 + 1) * 1024].rearrange("p (t c) -> p t c", c=P),
                             in_=hq_h.rearrange("(t p) c -> p t c", p=P)[:, part4 * 8:(part4 + 1) * 8,
                                                                        m * P:(m + 1) * P]).then_inc(hld.sem(m), 16)
            t.wait_ge(hld.sem(m).h, 64 * (m // 2 + 1))
            ps0 = psA if m % 2 == 0 else psB
            ps1 = psC if m % 2 == 0 else psD
            for Ko in range(OT):
                t.matmul(ps0[:], hs[:, Ko * P:(Ko + 1) * P], featB[:, Ko * D: Ko * D + 512],
                         start=(Ko == 0), stop=(Ko == OT - 1))
            for Ko in range(OT):
                t.matmul(ps1[:], hs[:, Ko * P:(Ko + 1) * P], featB[:, Ko * D + 512: Ko * D + 1024],
                         start=(Ko == 0), stop=(Ko == OT - 1))
            if m >= 1:
                s.wait_ge(gst.h, 16 * m)
            s.activation(out=gout[:, 0:512], in_=ps0[:], func=AF.Silu, scale=gsm[:, 5:6])
            s.activation(out=gout[:, 512:1024], in_=ps1[:], func=AF.Silu, scale=gsm[:, 5:6]).then_inc(tev, 1)
            sy.wait_ge(tev.h, tev.n)
            sy.dma_start(out=g0_o[m * P:(m + 1) * P, :], in_=gout[:]).then_inc(gst, 16)
        sy.wait_ge(gst.h, gst.n)
        return

    class _NS:
        pass
    _NS.build = staticmethod(build)
    _NS.emit = staticmethod(emit)
    return _NS

def _mk_s3():
    P = 128
    TLOC = 4096
    NT = TLOC // P
    D = 1024


    def build(do_ln=True):
        kb = KB()
        dio = {"xa": kb.dram("xa", [TLOC, D]), "sb": kb.dram("sb", [TLOC, D]),
               "gb": kb.dram("gb", [TLOC, D]),
               "xo": kb.dram("xo", [TLOC, D], kind="ExternalOutput")}
        if do_ln:
            dio["xnout"] = kb.dram("xnout", [TLOC, D], kind="ExternalOutput")
        emit(kb, dio, do_ln=do_ln)
        return kb


    def emit(kb, dio, do_ln=True, sems=None):
        sy, t, v, s, g = kb.sy, kb.t, kb.v, kb.s, kb.g
        xa = dio["xa"]; sb = dio["sb"]; gb = dio["gb"]
        xo_o = dio["xo"]
        if do_ln:
            xn_o = dio["xnout"]

        xt = kb.sbuf("xt", [P, 2 * D])
        st_ = kb.sbuf("st_", [P, 2 * D])
        gt_ = kb.sbuf("gt_", [P, 2 * D])
        x1t = kb.sbuf("x1t", [P, 2 * D])
        stp = kb.sbuf("stp", [P, 16])
        cst = kb.sbuf("cst", [P, 2])
        if do_ln:
            sq = kb.sbuf("sq", [P, D])
            xnb = kb.sbuf("xnb", [P, 2 * D])

        if sems is None:
            sems = {"xld": DmaPair(kb, "xld"), "x1s": DmaPair(kb, "x1s"),
                    "xns": DmaPair(kb, "xns") if do_ln else None,
                    "cmb": kb.sem("cmb"), "lnd": kb.sem("lnd")}
        xld, x1s, xns = sems["xld"], sems["x1s"], sems["xns"]
        cmb, lnd = sems["cmb"], sems["lnd"]
        bxld = [xld.s[0].n, xld.s[1].n]
        bx1s = [x1s.s[0].n, x1s.s[1].n]
        bxns = [xns.s[0].n, xns.s[1].n] if xns is not None else None
        bcmb, blnd = cmb.n, lnd.n

        g.memset(cst[:, 0:1], 1e-5)
        g.memset(cst[:, 1:2], -0.5)
        kb.barrier()

        for m in range(NT):
            par = m % 2
            xp = xt[:, par * D:(par + 1) * D]
            sp = st_[:, par * D:(par + 1) * D]
            gp = gt_[:, par * D:(par + 1) * D]
            x1p = x1t[:, par * D:(par + 1) * D]
            sc = stp[:, par * 8:(par + 1) * 8]
            if m >= 2:
                sy.wait_ge(cmb.h, bcmb + m - 1)
            sy.dma_start(out=xp, in_=xa[m * P:(m + 1) * P, :]).then_inc(xld.sem(m), 16)
            sy.dma_start(out=sp, in_=sb[m * P:(m + 1) * P, :]).then_inc(xld.sem(m), 16)
            sy.dma_start(out=gp, in_=gb[m * P:(m + 1) * P, :]).then_inc(xld.sem(m), 16)
            v.wait_ge(xld.sem(m).h, bxld[m % 2] + 48 * (m // 2 + 1))
            if m >= 2:
                v.wait_ge(x1s.sem(m).h, bx1s[m % 2] + 16 * ((m - 2) // 2 + 1))
            v.tensor_tensor(out=x1p, in0=sp, in1=gp, op=ALU.mult)
            v.tensor_tensor(out=x1p, in0=x1p, in1=xp, op=ALU.add).then_inc(cmb, 1)
            sy.wait_ge(cmb.h, bcmb + m + 1)
            sy.dma_start(out=xo_o[m * P:(m + 1) * P, :], in_=x1p).then_inc(x1s.sem(m), 16)
            if do_ln:
                xnp = xnb[:, par * D:(par + 1) * D]
                v.reduce_sum(out=sc[:, 0:1], in_=x1p, axis=AX.X)
                v.tensor_scalar(out=sc[:, 1:2], in0=sc[:, 0:1], scalar1=-1.0 / D, scalar2=None, op0=ALU.mult)
                s.activation(out=sq[:], in_=x1p, func=AF.Square, bias=sc[:, 1:2], accum_out=sc[:, 2:3])
                v.tensor_scalar(out=sc[:, 3:4], in0=sc[:, 2:3], scalar1=1.0 / D, scalar2=None, op0=ALU.mult)
                s.activation(out=sc[:, 4:5], in_=sc[:, 3:4], func=AF.Ln, bias=cst[:, 0:1])
                s.activation(out=sc[:, 5:6], in_=sc[:, 4:5], func=AF.Exp, scale=cst[:, 1:2])
                v.tensor_tensor(out=sc[:, 6:7], in0=sc[:, 1:2], in1=sc[:, 5:6], op=ALU.mult)
                if m >= 2:
                    s.wait_ge(xns.sem(m).h, bxns[m % 2] + 16 * ((m - 2) // 2 + 1))
                s.activation(out=xnp, in_=x1p, func=AF.Identity, scale=sc[:, 5:6], bias=sc[:, 6:7]).then_inc(lnd, 1)
                sy.wait_ge(lnd.h, blnd + m + 1)
                sy.dma_start(out=xn_o[m * P:(m + 1) * P, :], in_=xnp).then_inc(xns.sem(m), 16)
        for sm in x1s.s + (xns.s if (do_ln and xns is not None) else []):
            sy.wait_ge(sm.h, sm.n)
        return sems

    class _NS:
        pass
    _NS.build = staticmethod(build)
    _NS.emit = staticmethod(emit)
    return _NS

def _mk_s4():
    P = 128
    TLOC = 4096
    NT = 32
    D = 1024
    DT = 8
    H = 16
    DK = 64
    C = 128
    NCH = 16
    UPTO = int(os.environ.get("S4_UPTO", "9"))
    SUB = int(os.environ.get("S4_SUB", "9"))
    RB = int(os.environ.get("S4_RB", "15"))
    RC = int(os.environ.get("S4_RC", "3"))
    GRPS = int(os.environ.get("S4_GRPS", "4"))


    def build():
        kb = KB()
        dio = {"xn": kb.dram("xn", [TLOC, D]), "Wq": kb.dram("Wq", [D, D]),
               "Wk": kb.dram("Wk", [D, D]), "Wv": kb.dram("Wv", [D, D]),
               "Wo": kb.dram("Wo", [D, D]), "wg": kb.dram("wg", [H]),
               "bg": kb.dram("bg", [H]), "wl": kb.dram("wl", [H]),
               "bl": kb.dram("bl", [H]),
               "s1": kb.dram("s1", [TLOC, D], kind="ExternalOutput")}
        emit(kb, dio)
        return kb


    def emit(kb, dio):
        xn_i = dio["xn"]; Wq_i = dio["Wq"]; Wk_i = dio["Wk"]; Wv_i = dio["Wv"]
        Wo_i = dio["Wo"]; wg_i = dio["wg"]; bg_i = dio["bg"]; wl_i = dio["wl"]
        bl_i = dio["bl"]; s1_o = dio["s1"]
        y_h = kb.dram("y_h", [D, TLOC], BF16,
                      kind="ExternalOutput" if os.environ.get("S4_DBG") else "Internal")

        xnF = kb.sbuf("xnF", [P, DT * TLOC], BF16)
        wqT = kb.sbuf("wqT", [P, DT * D], BF16)
        wkT = kb.sbuf("wkT", [P, DT * D], BF16)
        wvT = kb.sbuf("wvT", [P, DT * D], BF16)
        wstg = kb.sbuf("wstg", [P, D])
        wstgb = kb.sbuf("wstgb", [P, D], BF16)
        qt = kb.sbuf("qt", [P, NT * 256], BF16)
        qT = kb.sbuf("qTs", [DK, 4 * P], BF16)
        kt = kb.sbuf("kt", [P, NT * 256], BF16)
        vt = kb.sbuf("vt", [P, NT * 256], BF16)
        yt = kb.sbuf("yt", [P, NT * 256], BF16)
        kf32 = kb.sbuf("kf32", [P, 256])
        nrm = kb.sbuf("nrm", [P, 16])
        dec = kb.sbuf("dec", [P, 8])
        lrow = kb.sbuf("lrow", [P, 4])
        Lrow = kb.sbuf("Lrow", [P, NT * 4])
        negLu = kb.sbuf("negLu", [P, NT * 4])
        crow = kb.sbuf("crow", [P, NT * 4])
        pirow = kb.sbuf("pirow", [P, NT * 4])
        pcrow = kb.sbuf("pcrow", [P, NT * 4])
        wgb = kb.sbuf("wgb", [P, 4 * H])
        tri = kb.sbuf("tri", [P, P])
        mneg = kb.sbuf("mneg", [P, P])
        onesf = kb.sbuf("onesf", [P, P])
        identb = kb.sbuf("identb", [P, P], BF16)
        identf = kb.sbuf("identf", [P, P])
        ind4 = kb.sbuf("ind4", [4, 4 * P])
        LrowT = kb.sbuf("LrowT", [4, P])
        Ssta = kb.sbuf("Ssta", [DK, 4 * DK])
        Sb16 = kb.sbuf("Sb16", [DK, 4 * DK], BF16)
        xnt = kb.sbuf("xnt", [P, 2 * D])
        xnb = kb.sbuf("xnb", [P, D], BF16)
        cst = kb.sbuf("cst", [P, 2])
        Em = kb.sbuf("Em", [P, 4 * P])
        at = kb.sbuf("at", [P, 4 * P], BF16)
        vT = kb.sbuf("vTs", [DK, 4 * P], BF16)
        kc = kb.sbuf("kc", [P, 4 * DK], BF16)

        psQ = kb.psum("psQ", [P, 512])
        psKV = kb.psum("psKV", [P, 512])
        psK = psKV[:, 0:256]
        psV = psKV[:, 256:512]
        psLL = kb.psum("psLL", [P, 16])
        psL = psLL[:, 0:8]
        psLC = psLL[:, 8:16]
        psB = kb.psum("psB", [P, 4 * P])
        psA = kb.psum("psA", [P, 4 * P])
        psYY = kb.psum("psYY", [P, 512])
        psY = psYY[:, 0:256]
        psY2 = psYY[:, 256:512]
        psS = kb.psum("psS", [DK, 4 * DK])
        psTT = kb.psum("psTT", [P, 2 * P], BF16)
        psT = [psTT[:, 0:P], psTT[:, P:2 * P]]

        trev = at[:, 0:4 * P]
        sy, t, v, s, g = kb.sy, kb.t, kb.v, kb.s, kb.g

        ld = kb.sem("ld")
        wct = kb.sem("wct")
        xld = DmaPair(kb, "xld")
        wld = DmaPair(kb, "wld")
        tring = DmaRing(kb, "yst", 4)
        tev = kb.sem("tev")
        sst = kb.sem("sst")
        old = DmaPair(kb, "old")

        g.memset(onesf[:], 1.0)
        g.affine_select(identf[:], onesf[:], [[1, P]], ALU.is_equal, 0.0,
                        base=0, channel_multiplier=-1)
        v.tensor_copy(out=identb[:], in_=identf[:])
        g.memset(tri[:], 1.0)
        g.affine_select(tri[:], tri[:], [[1, P]], ALU.is_ge, 0.0,
                        base=0, channel_multiplier=-1)
        g.memset(mneg[:], 0.0)
        g.affine_select(mneg[:], mneg[:], [[1, P]], ALU.is_ge, -1e30,
                        base=0, channel_multiplier=-1)
        g.memset(ind4[:], 1.0)
        for hh in range(4):
            g.affine_select(ind4[0:4, hh * P:(hh + 1) * P], ind4[0:4, hh * P:(hh + 1) * P],
                            [[0, P]], ALU.is_equal, 0.0, base=-hh, channel_multiplier=1)
        for ci, w_i in enumerate((wg_i, bg_i, wl_i, bl_i)):
            sy.dma_start(out=wgb[:, ci * H:(ci + 1) * H],
                         in_=w_i.unsqueeze(0).broadcast_to([P, H])).then_inc(ld, 16)
        g.memset(cst[:, 0:1], 0.5)

        def hpart(hh):
            b = (hh % 2) * DK
            return slice(b, b + DK)

        def hcol(hh, w=DK):
            b = (hh // 2) * w
            return slice(b, b + w)
        wti = [0]

        def wtranspose(w_i, wdst):
            for et in range(DT):
                ti = wti[0]
                wti[0] += 1
                stg = wstg
                if ti >= 1:
                    sy.wait_ge(wct.h, ti)
                sy.dma_start(out=stg[:], in_=w_i[et * P:(et + 1) * P, :]).then_inc(wld.sem(ti), 16)
                v.wait_ge(wld.sem(ti).h, wld.target(ti))
                v.tensor_copy(out=wstgb[:], in_=stg[:]).then_inc(wct, 1)
                for dt in range(DT):
                    slot = (ti * DT + dt) % 2
                    t.transpose(psT[slot], wstgb[:, dt * P:(dt + 1) * P], identb[:])
                    v.tensor_copy(out=wdst[:, dt * D + et * P: dt * D + (et + 1) * P],
                                  in_=psT[slot])

        for w_i, wdst in ((Wq_i, wqT), (Wk_i, wkT), (Wv_i, wvT)):
            wtranspose(w_i, wdst)
        xcv = kb.sem("xcv")
        for m in range(NT):
            xt = xnt[:, (m % 2) * D:(m % 2 + 1) * D]
            if m >= 2:
                sy.wait_ge(xcv.h, m - 1)
            sy.dma_start(out=xt, in_=xn_i[m * P:(m + 1) * P, :]).then_inc(xld.sem(m), 16)
            v.wait_ge(xld.sem(m).h, xld.target(m))
            v.tensor_copy(out=xnb[:], in_=xt).then_inc(xcv, 1)
            for b in range(DT):
                slot = (m * DT + b) % 2
                t.transpose(psT[slot], xnb[:, b * P:(b + 1) * P], identb[:])
                v.tensor_copy(out=xnF[:, b * TLOC + m * P: b * TLOC + (m + 1) * P],
                              in_=psT[slot])
        sy.wait_ge(ld.h, ld.n)
        kb.barrier()
        if UPTO < 1:
            return

        for gg in range(GRPS):
            ecol = gg * 256
            if gg > 0:
                for sm in tring.s:
                    if sm.n:
                        v.wait_ge(sm.h, sm.n)
            for m in range(NT):
                for Kt in range(DT):
                    t.matmul(psQ[:, 0:256], xnF[:, Kt * TLOC + m * P:(Kt * TLOC + (m + 1) * P)],
                             wqT[:, Kt * D + ecol: Kt * D + ecol + 256],
                             start=(Kt == 0), stop=(Kt == DT - 1))
                v.tensor_copy(out=qt[:, m * 256:(m + 1) * 256], in_=psQ[:, 0:256])
            for m in range(NT):
                for Kt in range(DT):
                    t.matmul(psK, xnF[:, Kt * TLOC + m * P:(Kt * TLOC + (m + 1) * P)],
                             wkT[:, Kt * D + ecol: Kt * D + ecol + 256],
                             start=(Kt == 0), stop=(Kt == DT - 1))
                for Kt in range(DT):
                    t.matmul(psV, xnF[:, Kt * TLOC + m * P:(Kt * TLOC + (m + 1) * P)],
                             wvT[:, Kt * D + ecol: Kt * D + ecol + 256],
                             start=(Kt == 0), stop=(Kt == DT - 1))
                v.tensor_copy(out=vt[:, m * 256:(m + 1) * 256], in_=psV)
                v.tensor_copy(out=kf32[:], in_=psK)
                sqs = Em[:, 0:256]
                s.activation(out=sqs, in_=kf32[:], func=AF.Square)
                v.tensor_reduce(out=nrm[:, 0:4], in_=sqs.rearrange("p (h k) -> p h k", k=DK),
                                axis=AX.X, op=ALU.add)
                v.tensor_reduce(out=nrm[:, 4:8], in_=kf32[:].rearrange("p (h k) -> p h k", k=DK),
                                axis=AX.X, op=ALU.add)
                s.activation(out=nrm[:, 8:12], in_=nrm[:, 0:4], func=AF.Ln)
                s.activation(out=nrm[:, 8:12], in_=nrm[:, 8:12], func=AF.Exp, scale=cst[:, 0:1])
                v.tensor_scalar(out=nrm[:, 8:12], in0=nrm[:, 8:12], scalar1=1e-12, scalar2=None, op0=ALU.max)
                v.reciprocal(out=nrm[:, 8:12], in_=nrm[:, 8:12])
                v.tensor_tensor(out=kt[:, m * 256:(m + 1) * 256].rearrange("p (h k) -> p h k", k=DK),
                                in0=kf32[:].rearrange("p (h k) -> p h k", k=DK),
                                in1=bcast_free(nrm[:, 8:12], DK), op=ALU.mult)
                v.tensor_tensor(out=nrm[:, 12:16], in0=nrm[:, 4:8], in1=nrm[:, 8:12], op=ALU.mult)
                v.tensor_scalar(out=nrm[:, 12:16], in0=nrm[:, 12:16], scalar1=1.0 / DK, scalar2=None, op0=ALU.mult)
                hsl = slice(gg * 4, gg * 4 + 4)
                v.tensor_tensor(out=dec[:, 0:4], in0=nrm[:, 12:16], in1=wgb[:, 0 * H:1 * H][:, hsl], op=ALU.mult)
                v.tensor_tensor(out=dec[:, 0:4], in0=dec[:, 0:4], in1=wgb[:, 1 * H:2 * H][:, hsl], op=ALU.add)
                s.activation(out=dec[:, 0:4], in_=dec[:, 0:4], func=AF.Sigmoid)
                v.tensor_tensor(out=dec[:, 4:8], in0=nrm[:, 12:16], in1=wgb[:, 2 * H:3 * H][:, hsl], op=ALU.mult)
                v.tensor_tensor(out=dec[:, 4:8], in0=dec[:, 4:8], in1=wgb[:, 3 * H:4 * H][:, hsl], op=ALU.add)
                s.activation(out=dec[:, 4:8], in_=dec[:, 4:8], func=AF.Sigmoid)
                v.tensor_scalar(out=dec[:, 4:8], in0=dec[:, 4:8], scalar1=-1.0, scalar2=1.0,
                                op0=ALU.mult, op1=ALU.add)
                v.tensor_tensor(out=dec[:, 0:4], in0=dec[:, 0:4], in1=dec[:, 4:8], op=ALU.mult)
                v.tensor_scalar(out=dec[:, 4:8], in0=dec[:, 0:4], scalar1=-1.0, scalar2=1.0,
                                op0=ALU.mult, op1=ALU.add)
                v.tensor_scalar(out=dec[:, 0:4], in0=dec[:, 0:4], scalar1=1e-35, scalar2=None, op0=ALU.max)
                s.activation(out=lrow[:, 0:4], in_=dec[:, 0:4], func=AF.Ln)
                s.activation(out=dec[:, 4:8], in_=dec[:, 4:8], func=AF.Ln)
                t.matmul(psL[:, 0:4], tri[:], lrow[:, 0:4], start=True, stop=True)
                t.matmul(psLC[:, 0:4], onesf[:], lrow[:, 0:4], start=True, stop=True)
                v.tensor_copy(out=Lrow[:, m * 4:(m + 1) * 4], in_=psL[:, 0:4])
                v.tensor_tensor(out=negLu[:, m * 4:(m + 1) * 4], in0=dec[:, 4:8],
                                in1=Lrow[:, m * 4:(m + 1) * 4], op=ALU.subtract)
                v.tensor_tensor(out=crow[:, m * 4:(m + 1) * 4], in0=psLC[:, 0:4],
                                in1=Lrow[:, m * 4:(m + 1) * 4], op=ALU.subtract)
                v.tensor_tensor(out=crow[:, m * 4:(m + 1) * 4], in0=crow[:, m * 4:(m + 1) * 4],
                                in1=dec[:, 4:8], op=ALU.add)
                s.activation(out=crow[:, m * 4:(m + 1) * 4], in_=crow[:, m * 4:(m + 1) * 4], func=AF.Exp)
                s.activation(out=pirow[:, m * 4:(m + 1) * 4], in_=Lrow[:, m * 4:(m + 1) * 4], func=AF.Exp)
                s.activation(out=pcrow[:, m * 4:(m + 1) * 4], in_=psLC[:, 0:4], func=AF.Exp)
            if SUB < 2:
                continue
            v.memset(Ssta[:], 0.0)
            v.memset(Sb16[:], 0.0)
            for m in range(NT):
                if m == NCH:
                    v.memset(Ssta[:], 0.0)
                    v.memset(Sb16[:], 0.0)
                t.transpose(psY2[0:4, 0:P], Lrow[:, m * 4:(m + 1) * 4], identf[:])
                v.tensor_copy(out=LrowT[0:4, :], in_=psY2[0:4, 0:P])
                for hh in range(4):
                    t.matmul(psB[:, hh * P:(hh + 1) * P], ind4[0:4, hh * P:(hh + 1) * P],
                             LrowT[0:4, :], start=True, stop=True)
                v.tensor_tensor(out=Em[:].rearrange("p (h i) -> p h i", h=4),
                                in0=psB[:].rearrange("p (h i) -> p h i", h=4),
                                in1=bcast_free(negLu[:, m * 4:(m + 1) * 4], P), op=ALU.add)
                v.tensor_tensor(out=Em[:].rearrange("p (h i) -> p h i", h=4),
                                in0=Em[:].rearrange("p (h i) -> p h i", h=4),
                                in1=mneg[:].unsqueeze(1).broadcast_to([P, 4, P]),
                                op=ALU.add)
                s.activation(out=Em[:], in_=Em[:], func=AF.Exp)
                if RB < 2:
                    continue
                for hh in range(4):
                    slot = hh % 2
                    t.transpose(psT[slot][0:DK, 0:P],
                                vt[:, m * 256 + hh * DK: m * 256 + (hh + 1) * DK], identb[:])
                    v.tensor_copy(out=vT[0:DK, hh * P:(hh + 1) * P], in_=psT[slot][0:DK, 0:P])
                for hh in range(4):
                    slot = hh % 2
                    t.transpose(psT[slot][0:DK, 0:P],
                                qt[:, m * 256 + hh * DK: m * 256 + (hh + 1) * DK], identb[:])
                    v.tensor_copy(out=qT[0:DK, hh * P:(hh + 1) * P], in_=psT[slot][0:DK, 0:P])
                if RC >= 2:
                    for hh in range(4):
                        t.matmul(psA[:, hh * P:(hh + 1) * P], vT[0:DK, hh * P:(hh + 1) * P],
                                 qT[0:DK, hh * P:(hh + 1) * P],
                                 start=True, stop=True)
                if RC >= 3:
                    v.tensor_tensor(out=at[:], in0=psA[:], in1=Em[:], op=ALU.mult)
                if RB < 4:
                    continue
                for hh in range(4):
                    t.matmul(psY[:, hh * DK:(hh + 1) * DK], at[:, hh * P:(hh + 1) * P],
                             kt[:, m * 256 + hh * DK: m * 256 + (hh + 1) * DK],
                             start=True, stop=True)
                for hh in range(4):
                    t.matmul(psY2[:, hh * DK:(hh + 1) * DK],
                             qT[0:DK, hh * P:(hh + 1) * P],
                             Sb16[0:DK, hh * DK:(hh + 1) * DK],
                             start=True, stop=True)
                yint = Em[:, 256:512]
                v.tensor_copy(out=yint, in_=psY[:])
                for hh in range(4):
                    v.scalar_tensor_tensor(out=yt[:, m * 256 + hh * DK: m * 256 + (hh + 1) * DK],
                                           in0=psY2[:, hh * DK:(hh + 1) * DK],
                                           scalar=pirow[:, m * 4 + hh: m * 4 + hh + 1],
                                           in1=yint[:, hh * DK:(hh + 1) * DK],
                                           op0=ALU.mult, op1=ALU.add)
                if RB < 8:
                    continue
                for hh in range(4):
                    v.tensor_scalar(out=kc[:, hh * DK:(hh + 1) * DK],
                                    in0=kt[:, m * 256 + hh * DK: m * 256 + (hh + 1) * DK],
                                    scalar1=crow[:, m * 4 + hh: m * 4 + hh + 1],
                                    scalar2=None, op0=ALU.mult)
                for hh in range(4):
                    t.matmul(psS[0:DK, hh * DK:(hh + 1) * DK],
                             vt[:, m * 256 + hh * DK: m * 256 + (hh + 1) * DK],
                             kc[:, hh * DK:(hh + 1) * DK], start=True, stop=True)
                for hh in range(4):
                    v.scalar_tensor_tensor(out=Ssta[0:DK, hh * DK:(hh + 1) * DK],
                                           in0=Ssta[0:DK, hh * DK:(hh + 1) * DK],
                                           scalar=pcrow[0:DK, m * 4 + hh: m * 4 + hh + 1],
                                           in1=psS[0:DK, hh * DK:(hh + 1) * DK],
                                           op0=ALU.mult, op1=ALU.add)
                v.tensor_copy(out=Sb16[:], in_=Ssta[:])
            if SUB < 3:
                continue
            for m in range(NT):
                for b in range(2):
                    sem, slot = tring.issue()
                    t.transpose(psT[slot % 2], yt[:, m * 256 + b * P: m * 256 + (b + 1) * P], identb[:])
                    ev = trev[:, slot * P:(slot + 1) * P]
                    tring.wait_slot_free(kb.s, tring.cnt - 1)
                    s.copy(out=ev, in_=psT[slot % 2]).then_inc(tev, 1)
                    sy.wait_ge(tev.h, tev.n)
                    sy.dma_start(out=y_h[gg * 256 + b * P: gg * 256 + (b + 1) * P,
                                         m * P:(m + 1) * P], in_=ev).then_inc(sem, 16)
        kb.barrier()
        for sm in tring.s:
            sy.wait_ge(sm.h, sm.n)
        if UPTO < 2:
            return

        woT = wqT
        sos = xnt[:, 0:D]
        wtranspose(Wo_i, woT)
        kb.barrier()
        yst = vt[:, 0:2 * D]
        for m in range(NT):
            hs = yst[:, (m % 2) * D:(m % 2 + 1) * D]
            if m >= 2:
                sy.wait_ge(sst.h, 16 * (m - 1))
            for q4 in range(4):
                sy.dma_start(out=hs[:, q4 * 256:(q4 + 1) * 256].rearrange("p (kt c) -> p kt c", c=P),
                             in_=y_h.rearrange("(kt p) c -> p kt c", p=P)[:, q4 * 2:(q4 + 1) * 2,
                                                                         m * P:(m + 1) * P]).then_inc(old.sem(m), 16)
            t.wait_ge(old.sem(m).h, 64 * (m // 2 + 1))
            if m >= 1:
                v.wait_ge(sst.h, 16 * m)
            for half in range(2):
                ps_w = psQ
                for Kt in range(DT):
                    t.matmul(ps_w[:, 0:512], hs[:, Kt * P:(Kt + 1) * P],
                             woT[:, Kt * D + half * 512: Kt * D + (half + 1) * 512],
                             start=(Kt == 0), stop=(Kt == DT - 1))
                v.tensor_copy(out=sos[:, half * 512:(half + 1) * 512], in_=ps_w[:, 0:512]).then_inc(tev, 1)
            sy.wait_ge(tev.h, tev.n)
            sy.dma_start(out=s1_o[m * P:(m + 1) * P, :], in_=sos[:]).then_inc(sst, 16)
        sy.wait_ge(sst.h, sst.n)
        return

    class _NS:
        pass
    _NS.build = staticmethod(build)
    _NS.emit = staticmethod(emit)
    return _NS

def _mk_s5():
    P = 128
    TLOC = 4096
    NT = 32
    D = 1024
    DT = 8
    ES = 128
    GRP = 128
    GT = 32
    F = 32
    UPTO = int(os.environ.get("S5_UPTO", "9"))


    def build():
        kb = KB()
        dio = {"xn": kb.dram("xn", [TLOC, D]), "Wc": kb.dram("Wc", [D, ES]),
               "bc": kb.dram("bc", [ES]), "W1": kb.dram("W1", [ES * D, F]),
               "b1": kb.dram("b1", [ES, F]), "W2": kb.dram("W2", [ES * D, F]),
               "b2": kb.dram("b2", [ES, D]),
               "g1": kb.dram("g1", [TLOC, D], kind="ExternalOutput")}
        emit(kb, dio)
        return kb


    def emit(kb, dio):
        xn_i = dio["xn"]; Wc_i = dio["Wc"]; bc_i = dio["bc"]; W1_i = dio["W1"]
        b1_i = dio["b1"]; W2_i = dio["W2"]; b2_i = dio["b2"]; g1_o = dio["g1"]
        w1f_h = kb.dram("w1f_h", [ES * D, F], BF16, kind="Internal")
        w2t_h = kb.dram("w2t_h", [ES * F, D], BF16, kind="Internal")
        rp_h = kb.dram("rp_h", [ES * GT, D], BF16, kind="Internal")

        xhi = kb.sbuf("xhi", [P, NT * D], BF16)
        Gbuf = kb.sbuf("Gbuf", [P, DT * ES * GT], BF16)
        ztok = kb.sbuf("ztok", [P, P])
        ptok = kb.sbuf("ptok", [P, NT * P], BF16)
        pes = kb.sbuf("pes", [P, NT * P], BF16)
        xstg = kb.sbuf("xstg", [P, 2 * D])
        xlo = kb.sbuf("xlo", [P, D], BF16)
        xfs = kb.sbuf("xfs", [P, 2 * DT * P], BF16)
        whi = kb.sbuf("whi", [P, DT * P], BF16)
        wlo = kb.sbuf("wlo", [P, DT * P], BF16)
        wstg = kb.sbuf("wstg", [P, P])
        zT = kb.sbuf("zT", [P, P])
        idf = kb.sbuf("idf", [P, P])
        gmx = kb.sbuf("gmx", [P, 4])
        gmT = kb.sbuf("gmT", [4, P])
        indg = kb.sbuf("indg", [4, P])
        tiec = kb.sbuf("tiec", [P, 2])
        bcrow = kb.sbuf("bcrow", [1, 2 * ES], BF16)
        bcst = kb.sbuf("bcst", [1, ES])
        onesb = kb.sbuf("onesb", [1, P], BF16)
        ones128 = kb.sbuf("ones128", [P, P], BF16)
        identb = kb.sbuf("identb", [P, P], BF16)
        w1cs = kb.sbuf("w1cs", [P, 8 * DT * F], BF16)
        b1T = kb.sbuf("b1T", [F, ES])
        Ibuf = kb.sbuf("Ibuf", [P, 4 * 512], BF16)
        w2s2 = kb.sbuf("w2s2", [64, 2 * D], BF16)
        b2b = kb.sbuf("b2b", [P, D], BF16)

        iscr = kb.sbuf("iscr", [P, 512])
        rptev = kb.sbuf("rptev", [F, DT * P], BF16)
        rpst = kb.sbuf("rpst", [P, 4 * 128], BF16)
        rpg = xfs
        gout = xstg
        pA = kb.psum("pA", [P, 512])
        pB = kb.psum("pB", [P, 512])
        pC = kb.psum("pC", [P, 512])
        pG = [kb.psum(f"pG{i}", [P, 512]) for i in range(4)]
        psTT = kb.psum("psTT", [P, 2 * P], BF16)
        psT = [psTT[:, 0:P], psTT[:, P:2 * P]]

        sy, t, v, s, g = kb.sy, kb.t, kb.v, kb.s, kb.g

        ld = kb.sem("ld")
        ldb = kb.sem("ldb")
        ldb2 = kb.sem("ldb2")
        ldb3 = kb.sem("ldb3")
        ldc = kb.sem("ldc")
        xld = DmaPair(kb, "xld")
        wpre = DmaRing(kb, "wpre", 4)
        tev = kb.sem("tev")
        w1ld = DmaPair(kb, "w1ld")
        w2ld = DmaPair(kb, "w2ld")
        rld = DmaPair(kb, "rld")
        rpd = DmaPair(kb, "rpd")
        gst = DmaPair(kb, "gst")

        g.memset(ones128[:], 1.0)
        g.affine_select(identb[:], ones128[:], [[1, P]], ALU.is_equal, 0.0,
                        base=0, channel_multiplier=-1)
        v.tensor_copy(out=idf[:], in_=identb[:])
        g.memset(onesb[:], 1.0)
        g.iota(tiec[:, 0:1], [[0, 1]], channel_multiplier=1,
               allow_small_or_imprecise_dtypes=True)
        v.tensor_scalar(out=tiec[:, 1:2], in0=tiec[:, 0:1], scalar1=32.0, scalar2=None, op0=ALU.is_ge)
        v.scalar_tensor_tensor(out=tiec[:, 1:2], in0=tiec[:, 0:1], scalar=64.0, in1=tiec[:, 1:2],
                               op0=ALU.is_ge, op1=ALU.add)
        v.scalar_tensor_tensor(out=tiec[:, 1:2], in0=tiec[:, 0:1], scalar=96.0, in1=tiec[:, 1:2],
                               op0=ALU.is_ge, op1=ALU.add)
        v.tensor_scalar(out=tiec[:, 1:2], in0=tiec[:, 1:2], scalar1=-32.0, scalar2=None, op0=ALU.mult)
        v.tensor_tensor(out=tiec[:, 0:1], in0=tiec[:, 0:1], in1=tiec[:, 1:2], op=ALU.add)
        v.tensor_scalar(out=tiec[:, 0:1], in0=tiec[:, 0:1],
                        scalar1=float(np.float32(1e-6) / np.float32(31.0)), scalar2=None, op0=ALU.mult)
        g.memset(indg[:], 1.0)
        for gg4 in range(4):
            g.affine_select(indg[0:4, gg4 * 32:(gg4 + 1) * 32], indg[0:4, gg4 * 32:(gg4 + 1) * 32],
                            [[0, 32]], ALU.is_equal, 0.0, base=-gg4, channel_multiplier=1)
        for dt in range(DT):
            sy.wait(tev)
            sy.dma_start(out=wstg[:], in_=Wc_i[dt * P:(dt + 1) * P, :]).then_inc(ld, 16)
            v.wait_ge(ld.h, ld.n)
            v.tensor_copy(out=whi[:, dt * P:(dt + 1) * P], in_=wstg[:])
            v.tensor_tensor(out=zT[:], in0=wstg[:], in1=whi[:, dt * P:(dt + 1) * P], op=ALU.subtract)
            v.tensor_copy(out=wlo[:, dt * P:(dt + 1) * P], in_=zT[:]).then_inc(tev, 1)
        sy.dma_start(out=bcst[:], in_=bc_i.unsqueeze(0)).then_inc(ldb, 16)
        v.wait_ge(ldb.h, ldb.n)
        v.tensor_copy(out=bcrow[0:1, 0:ES], in_=bcst[:])
        v.tensor_tensor(out=bcst[:], in0=bcst[:], in1=bcrow[0:1, 0:ES], op=ALU.subtract)
        v.tensor_copy(out=bcrow[0:1, ES:2 * ES], in_=bcst[:])
        sy.wait(tev)
        sy.dma_start(out=wstg[:, 0:F], in_=b1_i).then_inc(ldc, 16)
        t.wait_ge(ldc.h, ldc.n)
        t.transpose(pA[0:F, 0:P], wstg[:, 0:F], idf[:])
        v.tensor_copy(out=b1T[0:F, :], in_=pA[0:F, 0:P]).then_inc(tev, 1)
        sy.dma_start(out=xstg[:, 0:D], in_=b2_i).then_inc(ldb3, 16)
        v.wait_ge(ldb3.h, ldb3.n)
        v.tensor_copy(out=b2b[:], in_=xstg[:, 0:D]).then_inc(tev, 1)
        sy.wait_ge(ldb3.h, ldb3.n)
        sy.wait_ge(tev.h, tev.n)
        W1flat = W1_i.rearrange("r f -> (r f)")
        w1ff = w1f_h.rearrange("r f -> (r f)")
        NCH_W = 64
        CH = ES * D * F // NCH_W // P
        for c in range(NCH_W):
            sem, slot = wpre.issue()
            wpre.wait_slot_free(kb.sy, wpre.cnt - 1)
            stg = xstg[:, slot * 512:(slot + 1) * 512]
            bst = xfs[:, slot * 512:(slot + 1) * 512]
            sy.dma_start(out=stg, in_=W1flat[c * P * CH:(c + 1) * P * CH].rearrange("(p n) -> p n", p=P)).then_inc(sem, 16)
            v.wait_ge(sem.h, sem.n)
            v.tensor_copy(out=bst, in_=stg).then_inc(tev, 1)
            sy.wait_ge(tev.h, tev.n)
            sy.dma_start(out=w1ff[c * P * CH:(c + 1) * P * CH].rearrange("(p n) -> p n", p=P),
                         in_=bst).then_inc(sem, 16)
        for es in range(ES):
            for q in range(2):
                sem, slot = wpre.issue()
                wpre.wait_slot_free(kb.sy, wpre.cnt - 1)
                stg = xstg[:, 1024 + slot * 256: 1024 + slot * 256 + 128]
                bstg = xlo[:, slot * 128:(slot + 1) * 128]
                for d4 in range(4):
                    dt = q * 4 + d4
                    sy.dma_start(out=stg[:, d4 * F:(d4 + 1) * F],
                                 in_=W2_i[es * D + dt * P: es * D + (dt + 1) * P, :]).then_inc(sem, 16)
                v.wait_ge(sem.h, sem.n)
                v.tensor_copy(out=bstg, in_=stg).then_inc(tev, 1)
                t.wait_ge(tev.h, tev.n)
                t.transpose(psT[slot % 2][:], bstg, identb[:])
                s.wait_ge(sem.h, sem.n)
                s.copy(out=rpst[:, slot * 128:(slot + 1) * 128], in_=psT[slot % 2][:]).then_inc(tev, 1)
                sy.wait_ge(tev.h, tev.n)
                for d4 in range(4):
                    dt = q * 4 + d4
                    sy.dma_start(out=w2t_h[es * F:(es + 1) * F, dt * P:(dt + 1) * P],
                                 in_=rpst[d4 * F:(d4 + 1) * F, slot * 128:(slot + 1) * 128]).then_inc(sem, 16)
        sy.wait_ge(ld.h, ld.n)
        sy.wait_ge(ldb.h, ldb.n)
        sy.wait_ge(ldb2.h, ldb2.n)
        sy.wait_ge(ldb3.h, ldb3.n)
        sy.wait_ge(ldc.h, ldc.n)
        for sm in wpre.s:
            sy.wait_ge(sm.h, sm.n)
        kb.barrier()
        if UPTO < 1:
            return

        xcv = kb.sem("xcv")
        for m in range(NT):
            stg = xstg[:, (m % 2) * D:(m % 2 + 1) * D]
            if m >= 2:
                sy.wait_ge(xcv.h, 2 * (m - 1))
            sy.dma_start(out=stg, in_=xn_i[m * P:(m + 1) * P, :]).then_inc(xld.sem(m), 16)
            v.wait_ge(xld.sem(m).h, xld.target(m))
            v.tensor_copy(out=xhi[:, m * D:(m + 1) * D], in_=stg)
            v.tensor_tensor(out=stg, in0=stg, in1=xhi[:, m * D:(m + 1) * D], op=ALU.subtract)
            v.tensor_copy(out=xlo[:], in_=stg).then_inc(xcv, 2)
            for b in range(DT):
                slot = b % 2
                t.transpose(psT[slot], xhi[:, m * D + b * P: m * D + (b + 1) * P], identb[:])
                v.tensor_copy(out=xfs[:, b * P:(b + 1) * P], in_=psT[slot])
            for b in range(DT):
                slot = b % 2
                t.transpose(psT[slot], xlo[:, b * P:(b + 1) * P], identb[:])
                v.tensor_copy(out=xfs[:, (DT + b) * P:(DT + b + 1) * P], in_=psT[slot])
            for dt in range(DT):
                t.matmul(pA[:, 0:P], xfs[:, dt * P:(dt + 1) * P], whi[:, dt * P:(dt + 1) * P],
                         start=(dt == 0), stop=False)
            for dt in range(DT):
                t.matmul(pA[:, 0:P], xfs[:, dt * P:(dt + 1) * P], wlo[:, dt * P:(dt + 1) * P],
                         start=False, stop=False)
            for dt in range(DT):
                t.matmul(pA[:, 0:P], xfs[:, (DT + dt) * P:(DT + dt + 1) * P],
                         whi[:, dt * P:(dt + 1) * P], start=False, stop=False)
            t.matmul(pA[:, 0:P], onesb[0:1, 0:P], bcrow[0:1, 0:ES], start=False, stop=False)
            t.matmul(pA[:, 0:P], onesb[0:1, 0:P], bcrow[0:1, ES:2 * ES], start=False, stop=True)
            v.tensor_scalar(out=ztok[:], in0=pA[:, 0:P],
                            scalar1=tiec[:, 0:1], scalar2=None, op0=ALU.add)
            t.transpose(pB[:, 0:128], ztok[:], idf[:])
            v.tensor_copy(out=zT[:], in_=pB[:, 0:128])
            v.tensor_reduce(out=gmx[:, 0:4], in_=zT[:].rearrange("p (a b) -> p a b", b=32),
                            axis=AX.X, op=ALU.max)
            v.tensor_tensor(out=pes[:, m * P:(m + 1) * P].rearrange("p (a b) -> p a b", b=32),
                            in0=zT[:].rearrange("p (a b) -> p a b", b=32),
                            in1=bcast_free(gmx[:, 0:4], 32), op=ALU.is_equal)
            t.transpose(pB[0:4, 256:384], gmx[:, 0:4], idf[:])
            v.tensor_copy(out=gmT[0:4, :], in_=pB[0:4, 256:384])
            t.matmul(pC[:, 0:128], indg[0:4, :], gmT[0:4, :], start=True, stop=True)
            v.tensor_tensor(out=ptok[:, m * P:(m + 1) * P], in0=ztok[:],
                            in1=pC[:, 0:128], op=ALU.is_equal)
        kb.barrier()
        if UPTO < 2:
            return

        EG = ES * GT
        w1marks = []
        for gt in range(4):
            for gl in range(GT):
                gg = gt * GT + gl
                m, slot = divmod(gg, 4)
                pb = slot * 32
                for half in range(2):
                    for d4 in range(4):
                        dt = half * 4 + d4
                        t.matmul(pG[slot][:, d4 * ES:(d4 + 1) * ES],
                                 xhi[:, m * D + dt * P: m * D + (dt + 1) * P][pb:pb + 32, :],
                                 ptok[:, m * P:(m + 1) * P][pb:pb + 32, :],
                                 start=True, stop=True, tile_position=(pb, 0))
                    for d4 in range(4):
                        dt = half * 4 + d4
                        v.tensor_copy(
                            out=Gbuf[:, dt * EG:(dt + 1) * EG].rearrange(
                                "p (e g) -> p e g", g=GT)[:, :, gl:gl + 1].squeeze(2),
                            in_=pG[slot][:, d4 * ES:(d4 + 1) * ES])
            kb.barrier()
            if UPTO < 3:
                return
            W1BANK = (pA, pB, pC, pG[0])
            for es in range(ES):
                bank, band, colslot = es // 32, ((es // 16) % 2) * 32, (es % 16) * 32
                si = gt * ES + es
                slot8 = si % 8
                sem = w1ld.sem(si)
                sy.wait_ge(sem.h, sem.n)
                if si >= 8:
                    sy.wait_ge(tev.h, w1marks[si - 8])
                for h2 in range(2):
                    sy.dma_start(
                        out=w1cs[:, slot8 * DT * F + h2 * 4 * F: slot8 * DT * F + (h2 + 1) * 4 * F].rearrange("p (d f) -> p d f", f=F),
                        in_=w1f_h.rearrange("(e d p) f -> p e d f", p=P, d=DT)[:, es, h2 * 4:(h2 + 1) * 4, :]).then_inc(sem, 16)
                t.wait_ge(sem.h, sem.n)
                for dt in range(DT):
                    mm = t.matmul(W1BANK[bank][band:band + 32, colslot:colslot + 32],
                                  w1cs[:, slot8 * DT * F + dt * F: slot8 * DT * F + (dt + 1) * F],
                                  Gbuf[:, dt * EG + es * GT: dt * EG + (es + 1) * GT],
                                  start=(dt == 0), stop=(dt == DT - 1),
                                  tile_position=(0, band))
                    if dt == DT - 1:
                        mm.then_inc(tev, 1)
                        w1marks.append(tev.n)
            for bank in range(4):
                for bd in range(2):
                    es0 = bank * 32 + bd * 16
                    v.tensor_tensor(out=iscr[bd * 32:bd * 32 + 32, :].rearrange("p (s c) -> p s c", c=32),
                                    in0=W1BANK[bank][bd * 32:bd * 32 + 32, 0:512].rearrange("p (s c) -> p s c", c=32),
                                    in1=b1T[0:32, es0:es0 + 16].unsqueeze(2).broadcast_to([32, 16, 32]),
                                    op=ALU.add)
                v.tensor_relu(out=Ibuf[0:64, bank * 512:(bank + 1) * 512], in_=iscr[0:64, :])
            kb.barrier()
            if UPTO < 4:
                return
            for es in range(ES):
                bank, band, colslot = es // 32, ((es // 16) % 2) * 32, (es % 16) * 32
                par = es % 2
                sy.wait(tev) if es >= 2 else None
                sy.dma_start(out=w2s2[0:F, par * D:(par + 1) * D],
                             in_=w2t_h[es * F:(es + 1) * F, :]).then_inc(w2ld.sem(es), 16)
                sy.dma_start(out=w2s2[F:2 * F, par * D:(par + 1) * D],
                             in_=w2t_h[es * F:(es + 1) * F, :]).then_inc(w2ld.sem(es), 16)
                t.wait_ge(w2ld.sem(es).h, w2ld.sem(es).n)
                lb = band
                pW2 = pG[1] if lb == 0 else pG[2]
                for dt in range(DT):
                    t.matmul(pW2[:, dt * 32:(dt + 1) * 32],
                             w2s2[lb:lb + F, par * D + dt * P: par * D + (dt + 1) * P],
                             Ibuf[lb:lb + F, bank * 512 + colslot: bank * 512 + colslot + 32],
                             start=True, stop=True)
                v.tensor_copy(out=rpst[:, (es % 2) * 256:(es % 2 + 1) * 256], in_=pW2[:, 0:256]).then_inc(tev, 1)
                s.wait_ge(rpd.sem(es).h, rpd.sem(es).n)
                s.wait_ge(rpd.sem(es + 1).h, rpd.sem(es + 1).n)
                for dt in range(DT):
                    slot2 = dt % 2
                    t.transpose(psT[slot2][0:F, 0:P],
                                rpst[:, (es % 2) * 256 + dt * 32:(es % 2) * 256 + (dt + 1) * 32],
                                identb[:])
                    sc = s.copy(out=rptev[0:F, dt * P:(dt + 1) * P],
                                in_=psT[slot2][0:F, 0:P])
                    if dt == DT - 1:
                        sc.then_inc(tev, 1)
                sy.wait_ge(tev.h, tev.n)
                sy.dma_start(out=rp_h[es * GT:(es + 1) * GT, :],
                             in_=rptev[0:GT, :]).then_inc(rpd.sem(es), 16)
            kb.barrier()
            if UPTO < 5:
                return
            for sm in rpd.s:
                sy.wait_ge(sm.h, sm.n)
            for gl in range(GT):
                gg = gt * GT + gl
                m, slot = divmod(gg, 4)
                pb = slot * 32
                par = gl % 2
                sy.wait_ge(rld.sem(gl).h, rld.sem(gl).n)
                sy.dma_start(out=rpg[:, par * D:(par + 1) * D],
                             in_=rp_h[gl::GT, :]).then_inc(rld.sem(gl), 16)
                t.wait_ge(rld.sem(gl).h, rld.sem(gl).n)
                scat_ps = (pG[3], pC)
                for half in range(2):
                    t.matmul(scat_ps[half][0:F, 0:512],
                             pes[:, m * P:(m + 1) * P][:, pb:pb + 32],
                             rpg[:, par * D + half * 512: par * D + (half + 1) * 512],
                             start=True, stop=False)
                    t.matmul(scat_ps[half][0:F, 0:512],
                             pes[:, m * P:(m + 1) * P][:, pb:pb + 32],
                             b2b[:, half * 512:(half + 1) * 512],
                             start=False, stop=True)
                s.wait_ge(gst.sem(gl).h, gst.sem(gl).n)
                s.activation(out=gout[0:F, par * D:par * D + 512], in_=scat_ps[0][0:F, 0:512], func=AF.Silu)
                s.activation(out=gout[0:F, par * D + 512:par * D + D], in_=scat_ps[1][0:F, 0:512], func=AF.Silu).then_inc(tev, 1)
                sy.wait_ge(tev.h, tev.n)
                sy.dma_start(out=g1_o[gg * 32:(gg + 1) * 32, :],
                             in_=gout[0:F, par * D:(par + 1) * D]).then_inc(gst.sem(gl), 16)
            kb.barrier()
        for sm in gst.s:
            sy.wait_ge(sm.h, sm.n)
        return

    class _NS:
        pass
    _NS.build = staticmethod(build)
    _NS.emit = staticmethod(emit)
    return _NS

S1 = _mk_s1()
S2 = _mk_s2()
S3 = _mk_s3()
S4 = _mk_s4()
S5 = _mk_s5()


P, TLOC, D, O2, OI, ES, F, H = 128, 4096, 1024, 2048, 4096, 128, 32, 16


def build_fused(upto=6, dbg=False):
    kb = KB()
    ik = "ExternalOutput" if dbg else "Internal"
    x = kb.dram("x", [TLOC, D])
    aW1 = kb.dram("a_W1", [D, O2]); ab1 = kb.dram("a_b1", [O2])
    aW2 = kb.dram("a_W2", [D, D]); ab2 = kb.dram("a_b2", [D])
    acsw = kb.dram("a_csw", [16, 15])
    fgw = kb.dram("f_gw", [OI, D]); fuw = kb.dram("f_uw", [OI, D])
    fdw = kb.dram("f_dw", [D, OI])
    fsin = kb.dram("f_sin", [1]); fsind = kb.dram("f_sind", [1])
    qWq = kb.dram("q_Wq", [D, D]); qWk = kb.dram("q_Wk", [D, D])
    qWv = kb.dram("q_Wv", [D, D]); qWo = kb.dram("q_Wo", [D, D])
    qwg = kb.dram("q_wg", [H]); qbg = kb.dram("q_bg", [H])
    qwl = kb.dram("q_wl", [H]); qbl = kb.dram("q_bl", [H])
    mWc = kb.dram("m_Wc", [D, ES]); mbc = kb.dram("m_bc", [ES])
    mW1 = kb.dram("m_W1", [ES * D, F]); mb1 = kb.dram("m_b1", [ES, F])
    mW2 = kb.dram("m_W2", [ES * D, F]); mb2 = kb.dram("m_b2", [ES, D])
    xn_h = kb.dram("xn_h", [TLOC, D], kind=ik)
    s0_h = kb.dram("s0_h", [TLOC, D], kind=ik)
    g0_h = kb.dram("g0_h", [TLOC, D], kind=ik)
    x1_h = kb.dram("x1_h", [TLOC, D], kind=ik)
    xn1_h = kb.dram("xn1_h", [TLOC, D], kind=ik)
    s1a_h = kb.dram("s1a_h", [TLOC, D], kind=ik)
    g1_h = kb.dram("g1_h", [TLOC, D], kind=ik)
    x2 = kb.dram("x2", [TLOC, D], kind="ExternalOutput")

    def stage(prefix, fn):
        kb.prefix = prefix
        kb.scope_begin()
        r = fn()
        kb.prefix = ""
        kb.barrier()
        kb.scope_end()
        return r

    stage("a_", lambda: S1.emit(kb, {"x": x, "W1": aW1, "b1": ab1, "W2": aW2,
                                     "b2": ab2, "csw": acsw, "s0": s0_h,
                                     "xn": xn_h}, dumps=False))
    if upto < 2:
        return kb
    stage("f_", lambda: S2.emit(kb, {"xn": xn_h, "gw": fgw, "uw": fuw,
                                     "dw": fdw, "sin": fsin, "sind": fsind,
                                     "g0": g0_h}))
    if upto < 3:
        return kb
    csem = stage("c_", lambda: S3.emit(kb, {"xa": x, "sb": s0_h, "gb": g0_h,
                                            "xo": x1_h, "xnout": xn1_h},
                                       do_ln=True))
    if upto < 4:
        return kb
    stage("q_", lambda: S4.emit(kb, {"xn": xn1_h, "Wq": qWq, "Wk": qWk,
                                     "Wv": qWv, "Wo": qWo, "wg": qwg,
                                     "bg": qbg, "wl": qwl, "bl": qbl,
                                     "s1": s1a_h}))
    if upto < 5:
        return kb
    stage("m_", lambda: S5.emit(kb, {"xn": xn1_h, "Wc": mWc, "bc": mbc,
                                     "W1": mW1, "b1": mb1, "W2": mW2,
                                     "b2": mb2, "g1": g1_h}))
    if upto < 6:
        return kb
    stage("z_", lambda: S3.emit(kb, {"xa": x1_h, "sb": s1a_h, "gb": g1_h,
                                     "xo": x2}, do_ln=False, sems=csem))
    return kb


def host_inputs(inp, core):
    f32 = lambda a: np.ascontiguousarray(np.asarray(a, np.float32))
    return {"x": f32(np.asarray(inp["L0_attn_inputs"], np.float32)[2 * core:2 * core + 2].reshape(TLOC, D)),
            "a_W1": f32(inp["L0_attn_W1"]), "a_b1": f32(inp["L0_attn_b1"]),
            "a_W2": f32(inp["L0_attn_W2"]), "a_b2": f32(inp["L0_attn_b2"]),
            "a_csw": f32(inp["L0_attn_conv_shared_weight"]),
            "f_gw": f32(inp["L0_ffn_gate_w"]), "f_uw": f32(inp["L0_ffn_up_w"]),
            "f_dw": f32(inp["L0_ffn_down_w"]), "f_sin": f32(inp["L0_ffn_s_in"]),
            "f_sind": f32(inp["L0_ffn_s_in_down"]),
            "q_Wq": f32(inp["L1_attn_Wq"]), "q_Wk": f32(inp["L1_attn_Wk"]),
            "q_Wv": f32(inp["L1_attn_Wv"]), "q_Wo": f32(inp["L1_attn_Wo"]),
            "q_wg": f32(inp["L1_attn_W_gamma"]), "q_bg": f32(inp["L1_attn_b_gamma"]),
            "q_wl": f32(inp["L1_attn_W_lambda"]), "q_bl": f32(inp["L1_attn_b_lambda"]),
            "m_Wc": f32(np.asarray(inp["L1_ffn_Wc"], np.float32).reshape(D, ES)),
            "m_bc": f32(np.asarray(inp["L1_ffn_bc"], np.float32).reshape(ES)),
            "m_W1": f32(np.asarray(inp["L1_ffn_W1"], np.float32).reshape(ES * D, F)),
            "m_b1": f32(np.asarray(inp["L1_ffn_b1"], np.float32).reshape(ES, F)),
            "m_W2": f32(np.asarray(inp["L1_ffn_W2"], np.float32).reshape(ES * D, F)),
            "m_b2": f32(np.asarray(inp["L1_ffn_b2"], np.float32).reshape(ES, D))}



def kernel(**inputs):
    cores = list(range(8))
    nc = build_fused(upto=6).build()
    maps = [host_inputs(inputs, c) for c in cores]
    r = run_bass_kernel_spmd(nc, maps, core_ids=cores).results
    out = np.empty((16, 2048, 1024), np.float32)
    for c in cores:
        out[2 * c:2 * c + 2] = np.asarray(r[c]["x2"], np.float32).reshape(2, 2048, 1024)
    return out
